# version 30
# baseline (speedup 1.0000x reference)
import math
import numpy as np
import ml_dtypes
import concourse.bass as bass
import concourse.mybir as mybir
from concourse.bass_utils import run_bass_kernel_spmd

F32 = mybir.dt.float32
BF16 = mybir.dt.bfloat16
AF = mybir.ActivationFunctionType
ALU = mybir.AluOpType

NCORES = 8
NSEQ = 2
S = 2048
D = 1024
NH = 8
DH = 128
NE = 16
SE = 256
DQK = 128
NC8 = D // 128
NTC = 4
NTILE = S // 128

MU = math.sqrt(0.5)
STD = math.sqrt(0.25 * math.pi)
ALPHA2 = 1.0 / (S * STD)
BIG_SCALE = (DQK ** 0.5) / STD
BIG_SHIFT = -MU / STD
NEG = -30000.0
EPS_Z = 1e-18
LN_EPS = 1e-5

ENGS = ("tensor", "vector", "scalar", "gpsimd", "sync")
KB = 1024


def bucket_table():
    n = np.arange(S)
    max_exact = 16
    is_small = n < max_exact
    nf = np.maximum(n, 1).astype(np.float32)
    val = max_exact + (np.log(nf / max_exact) / math.log(128 / max_exact) * 16).astype(np.int32)
    val = np.minimum(val, 31)
    return np.where(is_small, n, val)


def host_constants():
    bt = bucket_table()
    L = S + 511
    oh = np.zeros((32, L), dtype=np.float32)
    msk = np.zeros((1, L), dtype=np.float32)
    d = np.arange(L) - 511
    msk[0, d < 0] = NEG
    valid = np.where(d >= 0)[0]
    oh[bt[d[valid]], valid] = 1.0
    import jax
    import jax.numpy as jnp
    half = DH // 2
    with jax.default_device(jax.devices("cpu")[0]):
        inv_freq_j = 1.0 / 10000.0 ** (jnp.arange(half, dtype=jnp.float32) / half)
        ang_j = jnp.arange(S, dtype=jnp.float32)[:, None] * inv_freq_j
        cos = np.asarray(jnp.cos(ang_j), dtype=np.float32)
        sin = np.asarray(jnp.sin(ang_j), dtype=np.float32)
    cosT = np.concatenate([cos.T, cos.T], axis=0)
    sinpm = np.concatenate([-sin.T, sin.T], axis=0)
    bf = ml_dtypes.bfloat16
    tri = np.zeros((128, 4 * 512), dtype=np.float32)
    cc = np.arange(128)[:, None]
    jj = np.arange(512)[None, :]
    for r in range(4):
        tri[:, r * 512:(r + 1) * 512] = (cc + 128 * r <= jj)
    return {
        "c_oh": oh,
        "c_msk": msk,
        "c_cosk": cosT.astype(np.float32),
        "c_sink": sinpm.astype(np.float32),
        "c_ones": np.ones((128, 128), dtype=bf),
        "c_idn": np.eye(128, dtype=np.float32).astype(bf),
        "c_adn": np.eye(128, dtype=np.float32)[::-1].copy().astype(bf),
        "c_tri": tri.astype(bf),
    }


class K:

    def __init__(self, nc):
        self.nc = nc
        self.q = {e: [] for e in ENGS}
        self.sems = {}
        self.cnt = {}
        self.dma16 = set()
        self.hwm = {e: {} for e in ENGS}
        self._keep = []

    def sem(self, name, dma=False):
        cm = self.nc.semaphore(name)
        h = cm.__enter__()
        self._keep.append(cm)
        self.sems[name] = h
        self.cnt[name] = 0
        if dma:
            self.dma16.add(name)
        return name

    def op(self, eng, fn, waits=(), inc=None):
        flat = []
        for w in waits:
            if w is None:
                continue
            if isinstance(w, tuple) and len(w) == 2 and isinstance(w[0], str):
                flat.append(w)
            else:
                for ww in w:
                    if ww is not None:
                        flat.append(ww)
        ev = None
        if inc is not None:
            self.cnt[inc] += 16 if inc in self.dma16 else 1
            ev = (inc, self.cnt[inc])
        self.q[eng].append((fn, tuple(flat), inc))
        return ev

    def mkpool(self, name, n=6):
        self.pools = getattr(self, "pools", {})
        self.pools[name] = {"sems": [self.sem(f"{name}{i}", dma=True) for i in range(n)],
                            "i": 0}

    def dma(self, qeng, out_ap, in_ap, pool, waits=(), slow=False):
        p = self.pools[pool]
        sem = p["sems"][p["i"] % len(p["sems"])]
        p["i"] += 1
        w = tuple(waits) + ((sem, self.cnt[sem]),)
        if slow:
            return self.op(qeng, lambda e, o=out_ap, i=in_ap: e.dma_start(
                o, i, allow_slow_non_contiguous=True), waits=w, inc=sem)
        return self.op(qeng, lambda e, o=out_ap, i=in_ap: e.dma_start(o, i),
                       waits=w, inc=sem)

    def flush(self, name=None):
        nc = self.nc
        if not any(self.q[e] for e in ENGS):
            return
        with nc.Block(name) as blk:
            for eng in ENGS:
                items = self.q[eng]
                if not items:
                    continue

                def body(engine, items=items, eng=eng):
                    hw = self.hwm[eng]
                    for fn, waits, inc in items:
                        for (s, v) in waits:
                            if hw.get(s, 0) < v:
                                engine.wait_ge(self.sems[s], v)
                                hw[s] = v
                        inst = fn(engine)
                        if inc is not None:
                            inst.then_inc(self.sems[inc], 16 if inc in self.dma16 else 1)
                getattr(blk, eng)(body)
        self.q = {e: [] for e in ENGS}
        for s in self.cnt:
            if s in self.dma16:
                mw = max((self.hwm[e].get(s, 0) for e in ENGS), default=0)
            else:
                mw = self.cnt[s]
            for e in ENGS:
                self.hwm[e][s] = max(self.hwm[e].get(s, 0), mw)


def build_program(dbg=(), upto=None):
    nc = bass.Bass("TRN2", target_bir_lowering=False, detect_race_conditions=False)
    k = K(nc)
    _n = [0]

    SB_BASE = 17408

    def sb(off_kb, name, shape, dtype):
        _n[0] += 1
        off = int(round(off_kb * KB)) + SB_BASE
        assert off % 64 == 0, (name, off_kb)
        assert off + shape[-1] * (2 if dtype == BF16 else 4) <= 229247, (name, off_kb)
        return nc.alloc_sbuf_tensor_at(f"{name}{_n[0]}", shape, dtype, offset=off)

    DI = lambda n, s, d: nc.dram_tensor(n, s, d, kind="ExternalInput")
    x_t = DI("x_t", [NSEQ, D, S], F32)
    wqkT_d = DI("wqkT", [D, DQK], F32)
    bqk_d = DI("bqk", [DQK], F32)
    gam_d = DI("gam", [2, DQK], F32)
    bet_d = DI("bet", [2, DQK], F32)
    wvT_d = DI("wvT", [D, D], F32)
    bv_d = DI("bv", [D], F32)
    rel_d = DI("rel", [32, 1], F32)
    w_inT_d = DI("w_inT", [D, 4112], F32)
    w_outT_d = DI("w_outT", [2 * D, D], F32)
    kT_d = DI("kT", [NH, NE, DH, SE], F32)
    vT_d = DI("vT", [NH, NE, SE, DH], F32)
    c_oh = DI("c_oh", [32, S + 511], F32)
    c_msk = DI("c_msk", [1, S + 511], F32)
    c_cosk = DI("c_cosk", [128, S], F32)
    c_sink = DI("c_sink", [128, S], F32)
    c_ones = DI("c_ones", [128, 128], BF16)
    c_idn = DI("c_idn", [128, 128], BF16)
    c_adn = DI("c_adn", [128, 128], BF16)
    c_tri = DI("c_tri", [128, 4 * 512], BF16)

    out_t = nc.dram_tensor("out_t", [NSEQ, D, S], F32, kind="ExternalOutput")

    fb_d = nc.dram_tensor("fb_d", [S + 511], BF16, kind="Internal")
    xsp_d = nc.dram_tensor("xsp_d", [128, NC8 * S], F32, kind="ExternalOutput")
    xn_d = nc.dram_tensor("xn_d", [128, NC8 * S], BF16, kind="ExternalOutput")
    g_d = nc.dram_tensor("g_d", [NE, S], BF16, kind="ExternalOutput")
    bc_d = nc.dram_tensor("bc_d", [8, S], F32, kind="ExternalOutput")
    rc_d = nc.dram_tensor("rc_d", [8, 512], F32, kind="ExternalOutput")
    rl_d = nc.dram_tensor("rl_d", [128, 256], F32, kind="ExternalOutput")

    dbg_outs = {}

    def dump(name, ap, shape, dtype, waits=()):
        if name not in dbg:
            return
        n = 2
        nm = "dbg_" + name
        while nm in dbg_outs.values():
            nm = f"dbg_{name}__{n}"
            n += 1
        dd = nc.dram_tensor(nm, list(shape), dtype, kind="ExternalOutput")
        dbg_outs[nm] = nm
        k.dma("sync", dd.ap(), ap, "Y", waits=waits)

    k.mkpool("Y", 8)
    k.mkpool("SD", 6)
    k.mkpool("GD", 8)
    Y, SD, GD = "Y", "SD", "GD"
    T = k.sem("T")
    V = k.sem("V")
    SC = k.sem("SC")
    G = k.sem("G")

    P0 = 192.0
    ones = sb(P0 + 0.000, "ones", [128, 128], BF16)
    idn = sb(P0 + 0.250, "idn", [128, 128], BF16)
    recs = sb(P0 + 0.500, "recs", [128, 512], F32)
    bqk_s = sb(P0 + 2.500, "bqk_s", [128, 1], F32)
    gb4 = sb(P0 + 2.625, "gb4", [128, 4], F32)
    bv_s = sb(P0 + 2.750, "bv_s", [1, D], BF16)
    rel_s = sb(P0 + 4.750, "rel_s", [32, 1], F32)
    gbraw = sb(P0 + 4.875, "gbraw", [128, 4], F32)
    stat = sb(P0 + 5.000, "stat", [1, 1536], F32)
    cst = sb(P0 + 11.000, "cst", [128, 4], F32)
    onesr = sb(P0 + 11.250, "onesr", [1, 128], F32)
    adn = sb(P0 + 11.750, "adn", [128, 128], BF16)
    colw = sb(P0 + 11.125, "colw", [1, 16], F32)

    PS = [nc.alloc_psum_tensor(f"ps{i}", [128, 512], F32) for i in range(7)]
    ps_tr = nc.alloc_psum_tensor("ps_tr", [128, 256], BF16)

    xres = sb(0, "xres", [128, NC8 * S], F32)
    v_tm = sb(64, "v_tm", [128, NTILE * D], BF16)
    attnT = sb(96, "attnT", [128, 16 * 512], BF16)
    attnT2 = sb(137.5, "attnT2", [128, 16 * 512], BF16)
    q_s = sb(112, "q_s", [128, S], BF16)
    k_s = sb(116, "k_s", [128, S], BF16)
    big = sb(120, "big", [128, S + 384], F32)
    xn1 = sb(129.5, "xn1", [128, NC8 * 512], BF16)
    bigrev = sb(129.5, "bigrev", [128, S + 384], BF16)
    wv_s = sb(137.5, "wv_s", [128, NC8 * D], BF16)
    wqk_s = sb(153.5, "wqk_s", [128, D], BF16)
    zeps = sb(155.5, "zeps", [128, 2 * 512], F32)
    gel = sb(159.5, "gel", [128, 2 * 512], F32)
    qksc = sb(163.5, "qksc", [128, 512], F32)
    sq32 = sb(163.5, "sq32", [128, 512], F32)
    ao1q = sb(165.5, "ao1q", [128, NC8 * 512], F32)
    xbfs = sb(165.5, "xbfs", [128, 2048], BF16)
    sqs = sb(169.5, "sqs", [128, 2048], BF16)
    mb = sb(181.5, "mb", [128, 512], F32)
    rb = sb(183.5, "rb", [128, 512], F32)
    t1s = sb(185.5, "t1s", [128, 512], F32)
    t2s = sb(187.5, "t2s", [128, 512], F32)
    wr32 = sb(189.5, "wr32", [128, NC8 * 16], F32)

    ffwd = sb(64, "ffwd", [128, NH * S], BF16)
    qf = sb(32, "qf", [128, NH * S], BF16)
    kh = sb(96, "kh", [128, 2 * NE * SE], BF16)
    vh = sb(112, "vh", [128, 2 * NE * 256], BF16)
    xn4 = sb(128, "xn4", [128, 2 * NC8 * 512], BF16)
    a_sb = sb(144, "a_sb", [128, 2 * 2 * S], BF16)
    g_b = sb(160, "g_b", [128, 2 * S], BF16)
    wstr = sb(168, "wstr", [128, NC8 * D], BF16)
    rlog = sb(184, "rlog", [128, 256], F32)
    r4 = sb(185, "r4", [128, 4], F32)
    ttmp = sb(185.25, "ttmp", [128, 16], F32)
    stmp2 = sb(185.5, "stmp2", [128, 16], F32)
    gtm = sb(186, "gtm", [128, 256], BF16)
    gfm = sb(186.5, "gfm", [16, S], BF16)
    qa = sb(32, "qa", [128, NH * S], BF16)
    ka = sb(128, "ka", [128, NH * S], BF16)
    va = sb(160, "va", [128, NTILE * D], BF16)
    xn5 = sb(0, "xn5", [128, 2 * NC8 * 512], BF16)
    wka = sb(16, "wka", [128, NH * NC8 * 128], BF16)
    tbl_c = sb(96, "tbl_c", [128, S], BF16)
    tbl_s = sb(100, "tbl_s", [128, S], BF16)
    kraw = sb(104, "kraw", [128, 2 * 512], BF16)
    ksw = sb(106, "ksw", [128, 2 * 512], BF16)
    xs_r = sb(108, "xs_r", [128, 2 * 512], BF16)
    wva = sb(110, "wva", [128, NC8 * D], BF16)
    ao2 = sb(96, "ao2", [128, NH * S], BF16)
    tri_s = sb(0, "tri_s", [128, 4 * 512], BF16)
    expT = sb(4, "expT", [128, 4 * 512], BF16)
    rcb = sb(8, "rcb", [128, 4 * 512], F32)
    wo_s = sb(128, "wo_s", [128, NC8 * 16 * 128], BF16)
    xq = sb(160, "xq", [128, NC8 * 512], F32)
    outq = sb(176, "outq", [128, NC8 * 512], BF16)
    t16 = sb(184, "t16", [128, 512], F32)
    t26 = sb(186, "t26", [128, 512], F32)
    sq6 = sb(188, "sq6", [128, 512], BF16)


    tmp_oh = sb(0, "tmp_oh", [32, S + 511], F32)
    tmp_msk = sb(10.5, "tmp_msk", [1, S + 511], F32)
    tmp_fb = sb(21, "tmp_fb", [1, S + 511], F32)

    k.dma("sync", ones[:], c_ones.ap(), Y)
    k.dma("sync", idn[:], c_idn.ap(), Y)
    k.dma("sync", adn[:], c_adn.ap(), Y)
    k.dma("sync", bqk_s[:], bqk_d.ap().rearrange("(p o) -> p o", o=1), Y)
    k.dma("sync", gbraw[:, 0:1], gam_d.ap()[0:1, :].rearrange("o p -> p o"), Y)
    k.dma("sync", gbraw[:, 1:2], bet_d.ap()[0:1, :].rearrange("o p -> p o"), Y)
    k.dma("sync", gbraw[:, 2:3], gam_d.ap()[1:2, :].rearrange("o p -> p o"), Y)
    ev_gbl = k.dma("sync", gbraw[:, 3:4], bet_d.ap()[1:2, :].rearrange("o p -> p o"), Y)
    k.dma("sync", rel_s[:], rel_d.ap(), Y)
    k.dma("sync", tmp_oh[:], c_oh.ap(), Y)
    ev_msk = k.dma("sync", tmp_msk[:], c_msk.ap(), Y)
    k.dma("gpsimd", bv_s[:], bv_d.ap().rearrange("(o n) -> o n", o=1), GD)
    k.op("gpsimd", lambda e: e.memset(cst[:, 0:1], float(LN_EPS)))
    k.op("gpsimd", lambda e: e.memset(cst[:, 1:2], 1.0))
    k.op("gpsimd", lambda e: e.memset(cst[:, 2:3], float(1.0 / math.sqrt(DH))))
    k.op("gpsimd", lambda e: e.memset(onesr[:], 1.0))
    nc.const_aps.register(F32, float(LN_EPS), cst[:, 0:1])
    nc.const_aps.register(F32, float(1.0 / math.sqrt(DH)), cst[:, 2:3])

    ev_gb = k.op("vector", lambda e: e.tensor_scalar(
        gb4[:, 0:2], gbraw[:, 0:2], float(ALPHA2), None, ALU.mult),
        waits=(ev_gbl,), inc=V)
    k.op("vector", lambda e: e.tensor_copy(gb4[:, 2:4], gbraw[:, 2:4]),
         waits=(ev_gb,), inc=V)

    nfb = S + 511
    ev_fparts = []
    prev = None
    for ci in range((nfb + 511) // 512):
        c0 = ci * 512
        c1 = min(nfb, c0 + 512)
        ps = PS[ci % 2]
        evm = k.op("tensor", lambda e, ps=ps, c0=c0, c1=c1: e.matmul(
            ps[0:1, 0:c1 - c0], rel_s[:], tmp_oh[:, c0:c1], start=True, stop=True),
            waits=(ev_msk, prev), inc=T)
        evs = k.op("scalar", lambda e, ps=ps, c0=c0, c1=c1: e.activation(
            tmp_fb[0:1, c0:c1], ps[0:1, 0:c1 - c0], AF.Copy,
            bias=float(BIG_SHIFT), scale=float(BIG_SCALE)), waits=(evm,), inc=SC)
        prev = evs
        ev_fparts.append(evs)
    ev_fb = k.op("vector", lambda e: e.tensor_tensor(
        tmp_fb[:], tmp_fb[:], tmp_msk[:], ALU.add), waits=tuple(ev_fparts), inc=V)
    ev_fbd = k.dma("gpsimd", fb_d.ap(), tmp_fb[:], GD, waits=(ev_fb,))
    ev_x_pend = [None, None]
    ev_x_pend[0] = [k.dma("sync", xres[:, c * S:(c + 1) * S],
                          x_t.ap()[0, c * 128:(c + 1) * 128, :], Y,
                          waits=(ev_fbd,))
                    for c in range(NC8)]
    k.flush("setup")
    if upto == "setup":
        return nc, dbg_outs

    stage_guard = [None, None, None, None]

    def stat_finalize(ps, row_m, row_sq, waits, ps_sq=None):
        if ps_sq is None:
            ps_sq = ps
        w = tuple(waits) + tuple(g for g in stage_guard if g is not None)
        e1 = k.op("vector", lambda e: e.tensor_scalar(
            stat[0:1, 0:512], ps[row_m:row_m + 1, :], 1.0 / D, None, ALU.mult),
            waits=w, inc=V)
        e2 = k.op("vector", lambda e: e.tensor_scalar(
            stat[0:1, 512:1024], ps_sq[row_sq:row_sq + 1, :], 1.0 / D, None, ALU.mult),
            waits=w, inc=V)
        e3 = k.op("vector", lambda e: e.tensor_tensor(
            stat[0:1, 1024:1536], stat[0:1, 0:512], stat[0:1, 0:512], ALU.mult),
            waits=(e1,), inc=V)
        e4 = k.op("vector", lambda e: e.tensor_tensor(
            stat[0:1, 512:1024], stat[0:1, 512:1024], stat[0:1, 1024:1536],
            ALU.subtract), waits=(e2, e3), inc=V)
        e5a = k.op("scalar", lambda e: e.activation(
            stat[0:1, 1024:1536], stat[0:1, 512:1024], AF.Sqrt, bias=float(LN_EPS)),
            waits=(e4,), inc=SC)
        e5 = k.op("vector", lambda e: e.reciprocal(
            stat[0:1, 512:1024], stat[0:1, 1024:1536]), waits=(e5a,), inc=V)
        return e1, e5, e2

    def bcast_psum(ps_m, ps_r, ev_m, ev_r, free_m=(), free_r=()):
        em = k.op("tensor", lambda e: e.matmul(
            ps_m[:], onesr[:], stat[0:1, 0:512], start=True, stop=True),
            waits=(ev_m,) + tuple(free_m), inc=T)
        er = k.op("tensor", lambda e: e.matmul(
            ps_r[:], onesr[:], stat[0:1, 512:1024], start=True, stop=True),
            waits=(ev_r,) + tuple(free_r), inc=T)
        stage_guard[0] = em
        stage_guard[1] = er
        return em, er

    def stage_bcast(row_md, row_rd, col0, mb_ap, rb_ap, ev_m, ev_r, free_evs=()):
        sl = slice(col0, col0 + 512)
        w1 = k.dma("sync", bc_d.ap()[row_md:row_md + 1, sl], stat[0:1, 0:512], SD,
                   waits=(ev_m,))
        w2 = k.dma("sync", bc_d.ap()[row_rd:row_rd + 1, sl], stat[0:1, 512:1024], SD,
                   waits=(ev_r,))
        stage_guard[0] = w1
        stage_guard[1] = w2
        stage_guard[2] = None
        stage_guard[3] = None
        b1 = k.dma("sync", mb_ap,
                   bc_d.ap()[row_md:row_md + 1, sl].to_broadcast((128, 512)), Y,
                   waits=(w1,) + tuple(free_evs))
        b2 = k.dma("sync", rb_ap,
                   bc_d.ap()[row_rd:row_rd + 1, sl].to_broadcast((128, 512)), Y,
                   waits=(w2,) + tuple(free_evs))
        return b1, b2

    ev_xnw = [[None] * NTC for _ in range(NSEQ)]
    ev_xsw = [[None] * NTC for _ in range(NSEQ)]

    for seq in range(NSEQ):
        if ev_x_pend[seq] is None:
            ev_x_pend[seq] = [k.dma("sync", xres[:, c * S:(c + 1) * S],
                                    x_t.ap()[seq, c * 128:(c + 1) * 128, :], Y)
                              for c in range(NC8)]
        ev_x = ev_x_pend[seq]
        ev_brev = k.dma("sync", bigrev[:], bass.AP(fb_d, 0, [[1, 128], [1, S + 384]]),
                        Y, waits=(ev_fbd, ev_x[0]))
        ev_big = None
        prev_bg = None
        for cb in range(5):
            c0 = cb * 512
            c1 = min(S + 384, c0 + 512)
            psb = PS[cb % 2]
            eb = k.op("tensor", lambda e, psb=psb, c0=c0, c1=c1: e.matmul(
                psb[:, 0:c1 - c0], adn[:], bigrev[:, c0:c1], start=True, stop=True),
                waits=(ev_brev, prev_bg), inc=T)
            ev_big = k.op("vector", lambda e, psb=psb, c0=c0, c1=c1: e.tensor_copy(
                big[:, c0:c1], psb[:, 0:c1 - c0]), waits=(eb,), inc=V)
            prev_bg = ev_big
        ev_cast0 = [k.dma("gpsimd", xbfs[:, c * 512:(c + 1) * 512],
                          x_t.ap()[seq, c * 128:(c + 1) * 128, 0:512], GD)
                    for c in range(4)]
        ev_wqk = k.dma("gpsimd", wqk_s[:].rearrange("p (c o) -> p c o", c=NC8),
                       wqkT_d.ap().rearrange("(c p) o -> p c o", p=128), GD)
        ev_wv = [None] * NC8
        for c in range(NC8):
            ev_wv[c] = k.dma("gpsimd", wv_s[:, c * D:(c + 1) * D],
                             wvT_d.ap()[c * 128:(c + 1) * 128, :], GD)

        ev_wr32 = k.dma("sync", wr32[:].rearrange("p (c o) -> p c o", c=NC8),
                        w_inT_d.ap()[:, 4096:4112].rearrange("(c p) o -> p c o", p=128), Y)
        ev_cw = None
        for c in range(NC8):
            ev_cw = k.op("tensor", lambda e, c=c: e.matmul(
                PS[6][0:1, 0:16], cst[:, 1:2], wr32[:, c * 16:(c + 1) * 16],
                start=(c == 0), stop=(c == NC8 - 1)), waits=(ev_wr32,), inc=T)
        ev_cw = k.op("vector", lambda e: e.tensor_copy(
            colw[:], PS[6][0:1, 0:16]), waits=(ev_cw,), inc=V)

        ev_q = [None] * NTC
        ev_k = [None] * NTC
        ev_vevac = [[None, None] for _ in range(NTILE)]
        prev_qk_evac = [ev_big, ev_big]
        prev_v_evac = [None, None]
        mbrb_free = ()
        last_xn1 = None
        prev_fin1 = ()
        cast_gate = [None] * 4
        for tc in range(NTC):
            ev_st = None
            for c in range(NC8):
                sl = slice((c % 4) * 512, (c % 4) * 512 + 512)
                if tc == 0 and c < 4:
                    ev_cast = ev_cast0[c]
                else:
                    ev_cast = k.dma("gpsimd", xbfs[:, sl],
                                    x_t.ap()[seq, c * 128:(c + 1) * 128,
                                             tc * 512:(tc + 1) * 512],
                                    GD, waits=(cast_gate[c % 4],))
                ev_sq = k.op("vector", lambda e, sl=sl: e.tensor_tensor(
                    sqs[:, sl], xbfs[:, sl], xbfs[:, sl], ALU.mult),
                    waits=(ev_cast,), inc=V)
                k.op("tensor", lambda e, c=c, sl=sl: e.matmul(
                    PS[4][0:1, :], ones[:, 0:1], xbfs[:, sl],
                    start=(c == 0), stop=(c == NC8 - 1)),
                    waits=(ev_cast,) + prev_fin1, inc=T)
                ev_st = k.op("tensor", lambda e, c=c, sl=sl: e.matmul(
                    PS[4][32:33, :], ones[:, 0:1], sqs[:, sl],
                    start=(c == 0), stop=(c == NC8 - 1)),
                    waits=(ev_sq,) + prev_fin1, inc=T)
                cast_gate[c % 4] = ev_st
            ev_m, ev_r, ev_pf = stat_finalize(PS[4], 0, 32, waits=(ev_st,))
            prev_fin1 = (ev_m, ev_pf)
            ev_mb, ev_rb = bcast_psum(PS[5], PS[6], ev_m, ev_r,
                                      free_m=mbrb_free, free_r=mbrb_free + ((ev_cw,) if tc == 0 else ()))
            ev_xn = [None] * NC8
            for c in range(NC8):
                e1 = k.op("vector", lambda e, c=c, tc=tc: e.tensor_tensor(
                    t1s[:], xres[:, c * S + tc * 512: c * S + tc * 512 + 512], PS[5][:],
                    ALU.subtract), waits=(ev_mb, ev_x[c]), inc=V)
                ev_xn[c] = k.op("vector", lambda e, c=c: e.tensor_tensor(
                    xn1[:, c * 512:(c + 1) * 512], t1s[:], PS[6][:], ALU.mult),
                    waits=(ev_rb, e1, ev_big if tc == 0 else None), inc=V)
            mbrb_free = (ev_xn[NC8 - 1],)
            last_xn1 = ev_xn[NC8 - 1]
            psq = PS[tc % 2]
            evq = None
            for c in range(NC8):
                evq = k.op("tensor", lambda e, psq=psq, c=c: e.matmul(
                    psq[:], wqk_s[:, c * 128:(c + 1) * 128], xn1[:, c * 512:(c + 1) * 512],
                    start=(c == 0), stop=(c == NC8 - 1)),
                    waits=(ev_xn[c], ev_wqk, prev_qk_evac[tc % 2]), inc=T)
            ev_silu = k.op("scalar", lambda e, psq=psq: e.activation(
                qksc[:], psq[:], AF.Silu, bias=bqk_s[:, 0:1]),
                waits=(evq, ev_k[tc - 1] if tc > 0 else None), inc=SC)
            prev_qk_evac[tc % 2] = ev_silu
            ev_q[tc] = k.op("vector", lambda e, tc=tc: e.tensor_scalar(
                q_s[:, tc * 512:(tc + 1) * 512], qksc[:], gb4[:, 0:1], gb4[:, 1:2],
                ALU.mult, ALU.add), waits=(ev_silu,), inc=V)
            ev_k[tc] = k.op("vector", lambda e, tc=tc: e.tensor_scalar(
                k_s[:, tc * 512:(tc + 1) * 512], qksc[:], gb4[:, 2:3], gb4[:, 3:4],
                ALU.mult, ALU.add), waits=(ev_silu,), inc=V)
            for jji in range(4):
                j = tc * 4 + jji
                for nh in range(2):
                    psv = PS[2 + nh]
                    evv = None
                    for c in range(NC8):
                        evv = k.op("tensor", lambda e, psv=psv, c=c, jji=jji, nh=nh: e.matmul(
                            psv[:], xn1[:, c * 512 + jji * 128: c * 512 + jji * 128 + 128],
                            wv_s[:, c * D + nh * 512: c * D + nh * 512 + 512],
                            start=(c == 0), stop=False),
                            waits=(ev_xn[c], ev_wv[c], prev_v_evac[nh]), inc=T)
                    evv = k.op("tensor", lambda e, psv=psv, nh=nh: e.matmul(
                        psv[:], ones[0:1, 0:128], bv_s[0:1, nh * 512:nh * 512 + 512],
                        start=False, stop=True), waits=(evv,), inc=T)
                    pv = k.op("scalar", lambda e, psv=psv, j=j, nh=nh: e.activation(
                        v_tm[:, j * D + nh * 512: j * D + nh * 512 + 512], psv[:], AF.Silu),
                        waits=(evv,), inc=SC)
                    prev_v_evac[nh] = pv
                    ev_vevac[j][nh] = pv
        dump("xn1", xn1[:], (128, NC8 * 512), BF16, waits=(last_xn1,))
        dump("q_s", q_s[:], (128, S), BF16, waits=(ev_q[3],))
        dump("k_s", k_s[:], (128, S), BF16, waits=(ev_k[3],))
        dump("v_tm", v_tm[:], (128, NTILE * D), BF16, waits=(ev_vevac[15][1],))
        k.flush(f"h1a_{seq}")
        if upto == "h1a" and seq == 0:
            return nc, dbg_outs

        ev_at_b = [[None] * 16, [None] * 16]
        prev_z = [None, None]
        prev_div = [None, None]
        ev_rlw = [None] * NTC
        prev_router_ev = None
        ev_applydone = None
        ev_aoe_all_prev = ()
        prev_fin_p2 = ()
        prev_fin_p3 = ()
        h1b_state = {"mbrb_free": mbrb_free, "prev_fin_p2": (), "prev_fin_p3": (),
                     "prev_router_ev": None, "ev_applydone": None,
                     "ev_aoe_all_prev": (), "prev_ao": [None, None]}

        def emit_simtile(qc, b, aoe_gate):
            at_dst = attnT if qc % 2 == 0 else attnT2
            psm = PS[b % 2]
            ev_sim = k.op("tensor", lambda e, psm=psm, b=b, qc=qc: e.matmul(
                psm[:], k_s[:, b * 128:(b + 1) * 128],
                q_s[:, qc * 512:(qc + 1) * 512], start=True, stop=True),
                waits=(ev_k[b // 4], ev_q[qc], prev_z[b % 2]), inc=T)
            off = qc * 512 - b * 128 + 384
            ev_z = k.op("vector", lambda e, psm=psm, b=b, off=off: e.scalar_tensor_tensor(
                zeps[:, (b % 2) * 512:(b % 2) * 512 + 512], psm[:], EPS_Z,
                big[:, off:off + 512], ALU.add, ALU.add),
                waits=(ev_sim, ev_big, prev_div[b % 2]), inc=V)
            prev_z[b % 2] = ev_z
            ev_g = k.op("scalar", lambda e, b=b: e.activation(
                gel[:, (b % 2) * 512:(b % 2) * 512 + 512],
                zeps[:, (b % 2) * 512:(b % 2) * 512 + 512], AF.Gelu),
                waits=(ev_z, prev_div[b % 2]), inc=SC)
            ev_rc1 = k.op("vector", lambda e, b=b: e.reciprocal(
                zeps[:, (b % 2) * 512:(b % 2) * 512 + 512],
                zeps[:, (b % 2) * 512:(b % 2) * 512 + 512]),
                waits=(ev_g,), inc=V)
            ev_at_b[qc % 2][b] = k.op("gpsimd", lambda e, at_dst=at_dst, b=b: e.tensor_tensor(
                at_dst[:, b * 512:(b + 1) * 512],
                gel[:, (b % 2) * 512:(b % 2) * 512 + 512],
                zeps[:, (b % 2) * 512:(b % 2) * 512 + 512], ALU.mult),
                waits=(ev_g, ev_rc1) + tuple(aoe_gate), inc=G)
            prev_div[b % 2] = ev_at_b[qc % 2][b]

        def emit_av(qc, aoe_gate_for_sim, router_pend):
            st = h1b_state
            nb = 4 * qc + 4
            at_src = attnT if qc % 2 == 0 else attnT2
            ev_src = ev_at_b[qc % 2]
            nb2 = 4 * (qc + 1) + 4 if qc < NTC - 1 else 0
            done = 0
            ev_aoe = [None] * NC8
            prev_ao = st["prev_ao"]
            ev_p2 = None
            ev_sq = None

            def emit_stats(dv):
                nonlocal ev_p2, ev_sq
                ev_sq = k.op("vector", lambda e, dv=dv: e.tensor_tensor(
                    sq32[:], ao1q[:, dv * 512:(dv + 1) * 512],
                    ao1q[:, dv * 512:(dv + 1) * 512], ALU.mult),
                    waits=(ev_aoe[dv], ev_p2, st["prev_router_ev"]), inc=V)
                k.op("tensor", lambda e, dv=dv: e.matmul(
                    PS[6][0:1, :], cst[:, 1:2], ao1q[:, dv * 512:(dv + 1) * 512],
                    start=(dv == 0), stop=(dv == NC8 - 1)),
                    waits=(ev_aoe[dv],) + tuple(st["prev_fin_p2"]), inc=T)
                ev_p2 = k.op("tensor", lambda e, dv=dv: e.matmul(
                    PS[6][32:33, :], cst[:, 1:2], sq32[:],
                    start=(dv == 0), stop=(dv == NC8 - 1)),
                    waits=(ev_sq,) + tuple(st["prev_fin_p2"]), inc=T)

            for dv in range(NC8):
                if dv == 3 and router_pend is not None:
                    emit_router(*router_pend)
                target = min(nb2, (nb2 * (dv + 1) + NC8 - 1) // NC8)
                while done < target:
                    emit_simtile(qc + 1, done, aoe_gate_for_sim)
                    done += 1
                psa = PS[2 + dv % 2]
                eva = None
                for b in range(nb):
                    eva = k.op("tensor", lambda e, psa=psa, at_src=at_src, b=b, dv=dv, nb=nb: e.matmul(
                        psa[:], v_tm[:, b * D + dv * 128: b * D + dv * 128 + 128],
                        at_src[:, b * 512:(b + 1) * 512],
                        start=(b == 0), stop=(b == nb - 1)),
                        waits=(ev_src[b], ev_vevac[b][dv // 4], prev_ao[dv % 2]), inc=T)
                ev_aoe[dv] = k.op("scalar", lambda e, psa=psa, dv=dv: e.activation(
                    ao1q[:, dv * 512:(dv + 1) * 512], psa[:], AF.Copy),
                    waits=(eva, h1b_state["ev_applydone"]), inc=SC)
                prev_ao[dv % 2] = ev_aoe[dv]
                if dv >= 4:
                    emit_stats(dv - 4)
            while done < nb2:
                emit_simtile(qc + 1, done, aoe_gate_for_sim)
                done += 1
            if router_pend is not None and NC8 <= 3:
                emit_router(*router_pend)
            for dv in range(4, NC8):
                emit_stats(dv)
            h1b_state["ev_aoe_all_prev"] = tuple(ev_aoe)
            return ev_aoe, ev_p2

        def emit_post(qc, ev_aoe, ev_p2):
            st = h1b_state
            ev_m, ev_r, ev_pf = stat_finalize(PS[6], 0, 32, waits=(ev_p2,))
            ev_mb, ev_rb = bcast_psum(PS[6], PS[5], ev_m, ev_r,
                                      free_m=(ev_pf,) + tuple(st["mbrb_free"]),
                                      free_r=tuple(st["mbrb_free"]) + tuple(st["prev_fin_p3"]))
            ev_xup = [None] * NC8
            ev_t2l = None
            for c in range(NC8):
                k.op("vector", lambda e, c=c: e.tensor_tensor(
                    t1s[:], ao1q[:, c * 512:(c + 1) * 512], PS[6][:], ALU.subtract),
                    waits=(ev_mb,), inc=V)
                ev_t2l = k.op("vector", lambda e: e.tensor_tensor(
                    t2s[:], t1s[:], PS[5][:], ALU.mult), waits=(ev_rb,), inc=V)
                ev_xup[c] = k.op("vector", lambda e, c=c, qc=qc: e.tensor_tensor(
                    xres[:, c * S + qc * 512: c * S + qc * 512 + 512],
                    xres[:, c * S + qc * 512: c * S + qc * 512 + 512], t2s[:], ALU.add),
                    inc=V)
            st["ev_applydone"] = ev_xup[NC8 - 1]
            st["mbrb_free"] = (ev_xup[NC8 - 1],)
            st["prev_fin_p2"] = (ev_m, ev_pf, ev_t2l)
            ev_p3 = None
            for c in range(NC8):
                ev_sq = k.op("vector", lambda e, c=c, qc=qc: e.tensor_tensor(
                    sq32[:], xres[:, c * S + qc * 512: c * S + qc * 512 + 512],
                    xres[:, c * S + qc * 512: c * S + qc * 512 + 512], ALU.mult),
                    waits=(ev_xup[c], ev_p3), inc=V)
                k.op("tensor", lambda e, c=c, qc=qc: e.matmul(
                    PS[5][0:1, :], cst[:, 1:2],
                    xres[:, c * S + qc * 512: c * S + qc * 512 + 512],
                    start=(c == 0), stop=(c == NC8 - 1)),
                    waits=(ev_xup[c], ev_t2l) + tuple(st["prev_fin_p3"]), inc=T)
                ev_p3 = k.op("tensor", lambda e, c=c: e.matmul(
                    PS[5][32:33, :], cst[:, 1:2], sq32[:],
                    start=(c == 0), stop=(c == NC8 - 1)),
                    waits=(ev_sq,) + tuple(st["prev_fin_p3"]), inc=T)
            ev_m2, ev_r2, ev_pf3 = stat_finalize(PS[5], 0, 32, waits=(ev_p3,))
            ev_mb2, ev_rb2 = bcast_psum(PS[5], PS[6], ev_m2, ev_r2,
                                        free_m=(ev_pf3,),
                                        free_r=(ev_t2l, ev_xup[NC8 - 1]))
            ev_xn2 = [None] * NC8
            for c in range(NC8):
                e1 = k.op("vector", lambda e, c=c, qc=qc: e.tensor_tensor(
                    t1s[:], xres[:, c * S + qc * 512: c * S + qc * 512 + 512], PS[5][:],
                    ALU.subtract), waits=(ev_mb2,), inc=V)
                ev_xn2[c] = k.op("vector", lambda e, c=c: e.tensor_tensor(
                    xn1[:, c * 512:(c + 1) * 512], t1s[:], PS[6][:], ALU.mult),
                    waits=(ev_rb2, e1), inc=V)
            st["mbrb_free"] = (ev_xn2[NC8 - 1],)
            st["prev_fin_p3"] = (ev_m2, ev_pf3, ev_xn2[NC8 - 1])
            ev_xnw[seq][qc] = k.dma(
                "sync",
                xn_d.ap().rearrange("p (c t) -> p c t", c=NC8)[:, :, qc * 512:(qc + 1) * 512],
                xn1[:].rearrange("p (c t) -> p c t", c=NC8), SD,
                waits=tuple(ev_xn2))
            ev_xsw[seq][qc] = k.dma(
                "sync",
                xsp_d.ap().rearrange("p (c t) -> p c t", c=NC8)[:, :, qc * 512:(qc + 1) * 512],
                xres[:].rearrange("p (c t) -> p c t", c=NC8)[:, :, qc * 512:(qc + 1) * 512],
                SD, waits=tuple(ev_xup))
            st["prev_fin_p2"] = st["prev_fin_p2"] + (ev_xn2[NC8 - 1],)
            return (qc, ev_m2, ev_r2, ev_xup)

        def emit_router(qc, ev_m2, ev_r2, ev_xup):
            st = h1b_state
            ev_ng = k.op("vector", lambda e: e.tensor_scalar(
                recs[0:1, 0:512], stat[0:1, 0:512], -1.0, None, ALU.mult),
                waits=(ev_m2,), inc=V)
            ev_rst = k.dma("sync", bc_d.ap()[5:6, qc * 512:(qc + 1) * 512],
                           stat[0:1, 512:1024], SD, waits=(ev_r2,))
            ev_rtm = k.dma("gpsimd", gbraw[:, 0:4],
                           bc_d.ap()[5:6, qc * 512:(qc + 1) * 512]
                           .rearrange("o (j p) -> p (o j)", p=128), GD,
                           waits=(ev_rst, st["prev_router_ev"]), slow=True)
            stage_guard[2] = ev_rst
            ev_rt_evac = st["prev_router_ev"]
            for jji in range(4):
                evr = None
                for c in range(NC8):
                    evr = k.op("tensor", lambda e, c=c, jji=jji, qc=qc: e.matmul(
                        PS[4][:, 0:16],
                        xres[:, c * S + qc * 512 + jji * 128: c * S + qc * 512 + jji * 128 + 128],
                        wr32[:, c * 16:(c + 1) * 16],
                        start=(c == 0), stop=False),
                        waits=(ev_xup[c], ev_cw, ev_rt_evac), inc=T)
                evr = k.op("tensor", lambda e, jji=jji: e.matmul(
                    PS[4][:, 0:16], recs[0:1, jji * 128:jji * 128 + 128],
                    colw[:], start=False, stop=True),
                    waits=(ev_ng,), inc=T)
                ev_rt_evac = k.op("scalar", lambda e, jji=jji: e.activation(
                    sq32[:, jji * 16:(jji + 1) * 16], PS[4][:, 0:16], AF.Copy,
                    scale=gbraw[:, jji:jji + 1]), waits=(evr, ev_rtm), inc=SC)
            ev_rlw[qc] = k.dma("sync", rl_d.ap()[:, qc * 64:(qc + 1) * 64],
                               sq32[:, 0:64], SD, waits=(ev_rt_evac,))
            st["prev_router_ev"] = ev_rlw[qc]
            stage_guard[3] = ev_ng

        for b0 in range(4):
            emit_simtile(0, b0, ())
        aoe_m1 = ()
        router_pend = None
        for qc in range(NTC):
            ev_aoe, ev_p2 = emit_av(qc, aoe_m1, router_pend)
            router_pend = emit_post(qc, ev_aoe, ev_p2)
            aoe_m1 = tuple(ev_aoe)
        emit_router(*router_pend)
        ev_aoe_all_prev = h1b_state["ev_aoe_all_prev"]
        k.op("sync", lambda e: e.nop(),
             waits=(ev_xsw[seq][NTC - 1], ev_xnw[seq][NTC - 1], ev_rlw[NTC - 1]))
        dump("ao1q", ao1q[:], (128, NC8 * 512), F32, waits=ev_aoe_all_prev)
        dump("attnT", attnT2[:], (128, 16 * 512), BF16, waits=(ev_at_b[1][15],))
        k.flush(f"h1b_{seq}")
        if upto == "h1b" and seq == 0:
            return nc, dbg_outs

        ev_wstr = k.dma("gpsimd", wstr[:].rearrange("p (c o) -> p c o", c=NC8),
                        w_inT_d.ap()[:, 0:D].rearrange("(c p) o -> p c o", p=128), GD)
        ev_rlog = k.dma("sync", rlog[:], rl_d.ap(), Y, waits=tuple(ev_rlw))
        ev_xl = [None, None]
        prev_xn_used = [None, None]
        prev_qf_evac = [None, None]
        for tc in range(NTC):
            par = tc % 2
            ev_xl[par] = k.dma(
                "sync", xn4[:, par * NC8 * 512:(par + 1) * NC8 * 512]
                .rearrange("p (c t) -> p c t", c=NC8),
                xn_d.ap().rearrange("p (c t) -> p c t", c=NC8)[:, :, tc * 512:(tc + 1) * 512],
                Y, waits=(ev_xnw[seq][tc], prev_xn_used[par]))
            last_mm = None
            for ofc in range(NC8):
                psq = PS[4 + ofc % 2]
                evq = None
                for c in range(NC8):
                    evq = k.op("tensor", lambda e, psq=psq, c=c, ofc=ofc, par=par: e.matmul(
                        psq[:], wstr[:, c * D + ofc * 128: c * D + ofc * 128 + 128],
                        xn4[:, par * NC8 * 512 + c * 512: par * NC8 * 512 + c * 512 + 512],
                        start=(c == 0), stop=(c == NC8 - 1)),
                        waits=(ev_xl[par], ev_wstr, prev_qf_evac[ofc % 2]), inc=T)
                last_mm = evq
                prev_qf_evac[ofc % 2] = k.op(
                    "vector", lambda e, psq=psq, ofc=ofc, tc=tc: e.tensor_copy(
                        qf[:, ofc * S + tc * 512: ofc * S + tc * 512 + 512], psq[:]),
                    waits=(evq,), inc=V)
            prev_xn_used[par] = last_mm
        ev_gtm = [None] * NTILE
        prev_gtm = None
        for j in range(NTILE):
            e1 = k.op("vector", lambda e, j=j: e.reduce_max(
                r4[:, 0:1], rlog[:, j * 16:(j + 1) * 16], axis=mybir.AxisListType.X),
                waits=(ev_rlog,), inc=V)
            e2 = k.op("vector", lambda e, j=j: e.tensor_scalar(
                ttmp[:], rlog[:, j * 16:(j + 1) * 16], r4[:, 0:1], None, ALU.is_equal),
                waits=(e1,), inc=V)
            e3 = k.op("vector", lambda e, j=j: e.scalar_tensor_tensor(
                ttmp[:], ttmp[:], -1.0e30, rlog[:, j * 16:(j + 1) * 16], ALU.mult, ALU.add),
                waits=(e2,), inc=V)
            e4 = k.op("vector", lambda e: e.reduce_max(
                r4[:, 1:2], ttmp[:], axis=mybir.AxisListType.X), waits=(e3,), inc=V)
            e5 = k.op("vector", lambda e, j=j: e.tensor_scalar(
                ttmp[:], rlog[:, j * 16:(j + 1) * 16], r4[:, 1:2], None, ALU.is_ge),
                waits=(e4,), inc=V)
            e6 = k.op("scalar", lambda e, j=j: e.activation(
                stmp2[:], rlog[:, j * 16:(j + 1) * 16], AF.Sigmoid),
                waits=(ev_rlog, prev_gtm), inc=SC)
            ev_gtm[j] = k.op("vector", lambda e, j=j: e.tensor_tensor(
                gtm[:, j * 16:(j + 1) * 16], ttmp[:], stmp2[:], ALU.mult),
                waits=(e5, e6), inc=V)
            prev_gtm = ev_gtm[j]
        ev_gfm = [None] * NTILE
        prev_tr = None
        for j in range(NTILE):
            etr = k.op("tensor", lambda e, j=j: e.transpose(
                ps_tr[0:16, 0:128], gtm[:, j * 16:(j + 1) * 16], idn[:]),
                waits=(ev_gtm[j], prev_tr), inc=T)
            ev_gfm[j] = k.op("vector", lambda e, j=j: e.tensor_copy(
                gfm[:, j * 128:(j + 1) * 128], ps_tr[0:16, 0:128]), waits=(etr,), inc=V)
            prev_tr = ev_gfm[j]
        ev_gw = k.dma("sync", g_d.ap(), gfm[:], SD, waits=tuple(ev_gfm))
        dump("g_d", gfm[:], (16, S), BF16, waits=(ev_gw,))

        ev_khl = [None, None]
        ev_vhl = [None, None]
        ev_gbl2 = [None, None]
        ev_odone_h = [None, None]
        ev_gelu_done = [prev_qf_evac[0], prev_qf_evac[1], None]
        ev_ffe_prev = ()
        ev_omm_done_e = [None, None]
        for h in range(NH):
            par = h % 2
            ev_khl[par] = k.dma(
                "gpsimd", kh[:, par * NE * SE:(par + 1) * NE * SE]
                .rearrange("p (e s) -> p e s", e=NE),
                kT_d.ap()[h].rearrange("e p s -> p e s"), GD,
                waits=(ev_odone_h[par],))
            ev_vhl[par] = k.dma(
                "gpsimd", vh[:, par * NE * 256:(par + 1) * NE * 256]
                .rearrange("p (e sc d) -> p e sc d", e=NE, sc=2),
                vT_d.ap()[h].rearrange("e (sc p) d -> p e sc d", p=128), GD,
                waits=(ev_odone_h[par],))
            ev_omm = None
            sct = 0
            for ei in range(NE):
                pe = ei % 2
                ev_gbl2[pe] = k.dma(
                    "sync", g_b[:, pe * S:(pe + 1) * S],
                    g_d.ap()[ei:ei + 1, :].to_broadcast((128, S)), Y,
                    waits=(ev_gw, ev_omm_done_e[pe]))
                ev_ap = {}
                for sc in range(2):
                    for qc4 in range(NTC):
                        pss = PS[4 + sct % 3]
                        ev_s = k.op("tensor", lambda e, pss=pss, sc=sc, qc4=qc4, ei=ei, h=h, par=par: e.matmul(
                            pss[:], kh[:, par * NE * SE + ei * SE + sc * 128:
                                       par * NE * SE + ei * SE + sc * 128 + 128],
                            qf[:, h * S + qc4 * 512: h * S + qc4 * 512 + 512],
                            start=True, stop=True),
                            waits=(ev_khl[par], ev_gelu_done[sct % 3], ev_ffe_prev), inc=T)
                        ev_gl = k.op("scalar", lambda e, pss=pss, sc=sc, qc4=qc4, pe=pe: e.activation(
                            a_sb[:, pe * 2 * S + sc * S + qc4 * 512:
                                 pe * 2 * S + sc * S + qc4 * 512 + 512], pss[:], AF.Gelu),
                            waits=(ev_s, ev_omm_done_e[pe]), inc=SC)
                        ev_gelu_done[sct % 3] = ev_gl
                        ev_ap[(sc, qc4)] = k.op("vector", lambda e, sc=sc, qc4=qc4, pe=pe: e.tensor_tensor(
                            a_sb[:, pe * 2 * S + sc * S + qc4 * 512:
                                 pe * 2 * S + sc * S + qc4 * 512 + 512],
                            a_sb[:, pe * 2 * S + sc * S + qc4 * 512:
                                 pe * 2 * S + sc * S + qc4 * 512 + 512],
                            g_b[:, pe * S + qc4 * 512: pe * S + qc4 * 512 + 512],
                            ALU.mult), waits=(ev_gl, ev_gbl2[pe]), inc=V)
                        sct += 1
                for sc in range(2):
                    for qc4 in range(NTC):
                        ev_omm = k.op("tensor", lambda e, sc=sc, qc4=qc4, ei=ei, pe=pe, par=par: e.matmul(
                            PS[qc4][:], vh[:, par * NE * 256 + ei * 256 + sc * 128:
                                           par * NE * 256 + ei * 256 + sc * 128 + 128],
                            a_sb[:, pe * 2 * S + sc * S + qc4 * 512:
                                 pe * 2 * S + sc * S + qc4 * 512 + 512],
                            start=(ei == 0 and sc == 0), stop=(ei == NE - 1 and sc == 1)),
                            waits=(ev_ap[(sc, qc4)], ev_vhl[par], ev_ffe_prev), inc=T)
                ev_omm_done_e[pe] = ev_omm
            ev_ffe = []
            for qc4 in range(NTC):
                ev_ffe.append(k.op("vector", lambda e, qc4=qc4, h=h: e.tensor_copy(
                    ffwd[:, h * S + qc4 * 512: h * S + qc4 * 512 + 512], PS[qc4][:]),
                    waits=(ev_omm,), inc=V))
            ev_ffe_prev = tuple(ev_ffe)
            ev_odone_h[par] = ev_ffe[-1]
        dump("qf", qf[:], (128, NH * S), BF16, waits=ev_ffe_prev[:1])
        dump("ffwd", ffwd[:], (128, NH * S), BF16, waits=ev_ffe_prev)
        k.flush(f"p4_{seq}")
        if upto == "p4" and seq == 0:
            return nc, dbg_outs

        def proj_rope(dst, col0, ev_prev_done):
            ev_wls = []
            for hh in range(NH):
                ev_wls.append(k.dma(
                    "gpsimd", wka[:, hh * D:(hh + 1) * D].rearrange("p (c o) -> p c o", c=NC8),
                    w_inT_d.ap()[:, col0 + hh * 128: col0 + hh * 128 + 128]
                    .rearrange("(c p) o -> p c o", p=128), GD,
                    waits=tuple(ev_prev_done)))
            ev_xl5 = [None, None]
            gate0 = ev_prev_done[0] if ev_prev_done else None
            prev_xn5_used = [gate0, gate0]
            prev_psum = [None] * 4
            prev_chain = [None, None]
            ci = 0
            ev_done = []
            for tc in range(NTC):
                par = tc % 2
                ev_xl5[par] = k.dma(
                    "sync", xn5[:, par * NC8 * 512:(par + 1) * NC8 * 512]
                    .rearrange("p (c t) -> p c t", c=NC8),
                    xn_d.ap().rearrange("p (c t) -> p c t", c=NC8)[:, :, tc * 512:(tc + 1) * 512],
                    Y, waits=(ev_xnw[seq][tc], prev_xn5_used[par]))
                last_mm = None
                for h in range(NH):
                    psp = PS[h % 4]
                    sl = slice((ci % 2) * 512, (ci % 2) * 512 + 512)
                    evp = None
                    for c in range(NC8):
                        evp = k.op("tensor", lambda e, psp=psp, c=c, h=h, par=par: e.matmul(
                            psp[:], wka[:, h * D + c * 128: h * D + c * 128 + 128],
                            xn5[:, par * NC8 * 512 + c * 512: par * NC8 * 512 + c * 512 + 512],
                            start=(c == 0), stop=(c == NC8 - 1)),
                            waits=(ev_xl5[par], ev_wls[h], prev_psum[h % 4]), inc=T)
                    last_mm = evp
                    ev_kr = k.op("scalar", lambda e, psp=psp, sl=sl: e.activation(
                        kraw[:, sl], psp[:], AF.Copy),
                        waits=(evp, prev_chain[ci % 2]), inc=SC)
                    es1 = k.op("scalar", lambda e, psp=psp, sl=sl: e.activation(
                        ksw[0:64, sl], psp[64:128, :], AF.Copy),
                        waits=(evp, prev_chain[ci % 2]), inc=SC)
                    es2 = k.op("scalar", lambda e, psp=psp, sl=sl: e.activation(
                        ksw[64:128, sl], psp[0:64, :], AF.Copy),
                        waits=(evp, prev_chain[ci % 2]), inc=SC)
                    prev_psum[h % 4] = es2
                    ev_X = k.op("vector", lambda e, tc=tc, sl=sl: e.tensor_tensor(
                        xs_r[:, sl], kraw[:, sl], tbl_c[:, tc * 512:(tc + 1) * 512],
                        ALU.mult), waits=(ev_kr,), inc=V)
                    ev_Y = k.op("vector", lambda e, tc=tc, sl=sl: e.tensor_tensor(
                        kraw[:, sl], ksw[:, sl], tbl_s[:, tc * 512:(tc + 1) * 512],
                        ALU.mult), waits=(es1, es2, ev_X), inc=V)
                    ev_o = k.op("vector", lambda e, h=h, tc=tc, sl=sl: e.tensor_tensor(
                        dst[:, h * S + tc * 512: h * S + tc * 512 + 512],
                        xs_r[:, sl], kraw[:, sl], ALU.add), waits=(ev_Y,), inc=V)
                    prev_chain[ci % 2] = ev_o
                    ev_done.append(ev_o)
                    ci += 1
                prev_xn5_used[par] = last_mm
            return ev_done

        ev_t1 = k.dma("gpsimd", tbl_c[:], c_cosk.ap(), GD)
        ev_t2 = k.dma("gpsimd", tbl_s[:], c_sink.ap(), GD)
        kv = k.op("vector", lambda e: e.nop(), waits=(ev_t1, ev_t2), inc=V)
        ev_ka_done = proj_rope(ka, 2048, (kv,))
        kv = k.op("vector", lambda e: e.nop(), waits=tuple(ev_ka_done[-1:]), inc=V)
        ev_qa_done = proj_rope(qa, 1024, (kv,))

        ev_wva = k.dma("gpsimd", wva[:].rearrange("p (c o) -> p c o", c=NC8),
                       w_inT_d.ap()[:, 3072:4096].rearrange("(c p) o -> p c o", p=128), GD,
                       waits=tuple(ev_qa_done[-1:]))
        ev_va = [[None, None] for _ in range(NTILE)]
        ev_xl5 = [None, None]
        prev_xn5v = [None, None]
        prev_vae = [None, None]
        for tc in range(NTC):
            par = tc % 2
            ev_xl5[par] = k.dma(
                "sync", xn5[:, par * NC8 * 512:(par + 1) * NC8 * 512]
                .rearrange("p (c t) -> p c t", c=NC8),
                xn_d.ap().rearrange("p (c t) -> p c t", c=NC8)[:, :, tc * 512:(tc + 1) * 512],
                Y, waits=(ev_xnw[seq][tc], prev_xn5v[par], tuple(ev_qa_done[-1:])))
            last_mm = None
            for jji in range(4):
                j = tc * 4 + jji
                for nh in range(2):
                    psv = PS[2 + nh]
                    evv = None
                    for c in range(NC8):
                        evv = k.op("tensor", lambda e, psv=psv, c=c, jji=jji, nh=nh, par=par: e.matmul(
                            psv[:], xn5[:, par * NC8 * 512 + c * 512 + jji * 128:
                                        par * NC8 * 512 + c * 512 + jji * 128 + 128],
                            wva[:, c * D + nh * 512: c * D + nh * 512 + 512],
                            start=(c == 0), stop=(c == NC8 - 1)),
                            waits=(ev_xl5[par], ev_wva, prev_vae[nh]), inc=T)
                    last_mm = evv
                    pv = k.op("scalar", lambda e, psv=psv, j=j, nh=nh: e.activation(
                        va[:, j * D + nh * 512: j * D + nh * 512 + 512], psv[:], AF.Copy),
                        waits=(evv,), inc=SC)
                    prev_vae[nh] = pv
                    ev_va[j][nh] = pv
            prev_xn5v[par] = last_mm
        dump("ka", ka[:], (128, NH * S), BF16, waits=tuple(ev_ka_done[-1:]))
        dump("qa", qa[:], (128, NH * S), BF16, waits=tuple(ev_qa_done[-1:]))
        dump("va", va[:], (128, NTILE * D), BF16, waits=(prev_vae[1],))
        k.flush(f"p5a_{seq}")
        if upto == "p5a" and seq == 0:
            return nc, dbg_outs

        ev_tri = k.dma("sync", tri_s[:], c_tri.ap(), Y)
        slot_free = [None] * 4
        prev_exp = [None, None]
        PU = [2, 3, 6]
        ev_uev = {2: None, 3: None, 6: None}
        den_guard = [None] * 6
        ev_rcb_used = [None] * 4
        prev_rec = [None] * 4
        last_ue = [None, None]
        pend_evac = None

        def emit_evac(pu, h, qc, hq_, ev_u_, ev_rb2_):
            ev_ue = k.op("vector", lambda e, pu=pu, h=h, qc=qc, hq=hq_: e.tensor_tensor(
                ao2[:, h * S + qc * 512: h * S + qc * 512 + 512], PS[pu][:],
                rcb[:, (hq % 4) * 512:(hq % 4) * 512 + 512], ALU.mult),
                waits=(ev_u_, ev_rb2_), inc=V)
            ev_uev[pu] = ev_ue
            ev_rcb_used[hq_ % 4] = ev_ue
            last_ue[hq_ % 2] = ev_ue

        hq = 0
        for qc in range(NTC):
            nb = 4 * qc + 4
            for h in range(NH):
                pu = PU[hq % 3]
                pd = 4 + (hq % 2)
                drow = 32 * ((hq // 2) % 3)
                ev_dn = None
                ev_u = None
                pend = None
                for b in range(nb + 1):
                    if b < nb:
                        pssc = PS[b % 2]
                        sl4 = b % 4
                        ev_sc2 = k.op("tensor", lambda e, pssc=pssc, b=b, h=h, qc=qc: e.matmul(
                            pssc[:], ka[:, h * S + b * 128: h * S + b * 128 + 128],
                            qa[:, h * S + qc * 512: h * S + qc * 512 + 512],
                            start=True, stop=True),
                            waits=(prev_exp[b % 2], ev_uev[pu]), inc=T)
                        ev_e = k.op("scalar", lambda e, pssc=pssc, sl4=sl4: e.activation(
                            expT[:, sl4 * 512:sl4 * 512 + 512], pssc[:], AF.Exp,
                            scale=float(1.0 / math.sqrt(DH))),
                            waits=(ev_sc2, slot_free[sl4]), inc=SC)
                        prev_exp[b % 2] = ev_e
                        if b >= nb - 4:
                            r = b - (nb - 4)
                            ev_e = k.op("vector", lambda e, sl4=sl4, r=r: e.tensor_tensor(
                                expT[:, sl4 * 512:sl4 * 512 + 512],
                                expT[:, sl4 * 512:sl4 * 512 + 512],
                                tri_s[:, r * 512:(r + 1) * 512], ALU.mult),
                                waits=(ev_e, ev_tri), inc=V)
                        cur = (b, sl4, ev_e)
                    else:
                        cur = None
                    if pend is not None:
                        pb, psl4, pev_e = pend
                        ev_dn = k.op("tensor", lambda e, pd=pd, drow=drow, sl4=psl4, b=pb, nb=nb: e.matmul(
                            PS[pd][drow:drow + 1, :], ones[:, 0:1],
                            expT[:, sl4 * 512:sl4 * 512 + 512],
                            start=(b == 0), stop=(b == nb - 1)),
                            waits=(pev_e, den_guard[hq % 6] if pb == 0 else None), inc=T)
                        ev_u = k.op("tensor", lambda e, pu=pu, sl4=psl4, b=pb, nb=nb, h=h: e.matmul(
                            PS[pu][:], va[:, b * D + h * 128: b * D + h * 128 + 128],
                            expT[:, sl4 * 512:sl4 * 512 + 512],
                            start=(b == 0), stop=(b == nb - 1)),
                            waits=(pev_e, ev_va[pb][h // 4]), inc=T)
                        slot_free[psl4] = ev_u
                    pend = cur
                if pend_evac is not None:
                    emit_evac(*pend_evac)
                rrow = 32 * (hq % 4)
                ev_rc = k.op("vector", lambda e, pd=pd, drow=drow, rrow=rrow: e.reciprocal(
                    recs[rrow:rrow + 1, :], PS[pd][drow:drow + 1, :]),
                    waits=(ev_dn, prev_rec[hq % 4]), inc=V)
                den_guard[hq % 6] = ev_rc
                ev_rw = k.dma("sync", rc_d.ap()[hq % 4:hq % 4 + 1, :],
                              recs[rrow:rrow + 1, :], SD, waits=(ev_rc,))
                ev_rb2 = k.dma("sync", rcb[:, (hq % 4) * 512:(hq % 4) * 512 + 512],
                               rc_d.ap()[hq % 4:hq % 4 + 1, :].to_broadcast((128, 512)), Y,
                               waits=(ev_rw, ev_rcb_used[hq % 4]))
                prev_rec[hq % 4] = ev_rb2
                pend_evac = (pu, h, qc, hq, ev_u, ev_rb2)
                hq += 1
        emit_evac(*pend_evac)
        dump("ao2", ao2[:], (128, NH * S), BF16, waits=(last_ue[0], last_ue[1]))
        k.flush(f"p5b_{seq}")
        if upto == "p5b" and seq == 0:
            return nc, dbg_outs

        ev_wol = [None] * NC8
        for ec in range(NC8):
            ev_wol[ec] = k.dma(
                "gpsimd", wo_s[:, ec * 16 * 128:(ec + 1) * 16 * 128]
                .rearrange("p (cc o) -> p cc o", cc=16),
                w_outT_d.ap()[:, ec * 128:(ec + 1) * 128]
                .rearrange("(cc p) o -> p cc o", p=128), GD)
        prev_oq = [None, None]
        prev_fin_p6 = ()
        prev_app6 = ()
        ev_out_final = None
        for qc in range(NTC):
            ev_xq = k.dma(
                "sync", xq[:].rearrange("p (c t) -> p c t", c=NC8),
                xsp_d.ap().rearrange("p (c t) -> p c t", c=NC8)[:, :, qc * 512:(qc + 1) * 512],
                Y, waits=(ev_xsw[seq][qc], ev_out_final))
            ev_oqe = [None] * NC8
            for ec in range(NC8):
                pso = PS[ec % 2]
                evo = None
                for cc in range(16):
                    src = ao2 if cc < 8 else ffwd
                    evo = k.op("tensor", lambda e, pso=pso, cc=cc, src=src, qc=qc, ec=ec: e.matmul(
                        pso[:], wo_s[:, ec * 16 * 128 + cc * 128: ec * 16 * 128 + cc * 128 + 128],
                        src[:, (cc % 8) * S + qc * 512: (cc % 8) * S + qc * 512 + 512],
                        start=(cc == 0), stop=(cc == 15)),
                        waits=(ev_wol[ec], prev_oq[ec % 2]), inc=T)
                ev_oqe[ec] = k.op("vector", lambda e, pso=pso, ec=ec: e.tensor_copy(
                    outq[:, ec * 512:(ec + 1) * 512], pso[:]), waits=(evo,), inc=V)
                prev_oq[ec % 2] = ev_oqe[ec]
            ev_p6 = None
            for ec in range(NC8):
                ev_sq = k.op("vector", lambda e, ec=ec: e.tensor_tensor(
                    sq6[:], outq[:, ec * 512:(ec + 1) * 512],
                    outq[:, ec * 512:(ec + 1) * 512], ALU.mult),
                    waits=(ev_oqe[ec], ev_p6), inc=V)
                k.op("tensor", lambda e, ec=ec: e.matmul(
                    PS[6][0:1, :], ones[:, 0:1], outq[:, ec * 512:(ec + 1) * 512],
                    start=(ec == 0), stop=(ec == NC8 - 1)),
                    waits=(ev_oqe[ec],) + prev_fin_p6, inc=T)
                ev_p6 = k.op("tensor", lambda e, ec=ec: e.matmul(
                    PS[6][32:33, :], ones[:, 0:1], sq6[:],
                    start=(ec == 0), stop=(ec == NC8 - 1)),
                    waits=(ev_sq,) + prev_fin_p6, inc=T)
            ev_m, ev_r, ev_pf = stat_finalize(PS[6], 0, 32, waits=(ev_p6,))
            ev_mb, ev_rb = bcast_psum(PS[6], PS[5], ev_m, ev_r,
                                      free_m=(ev_pf,) + prev_app6, free_r=prev_app6)
            ev_sts = [None] * NC8
            ev_t26l = None
            for c in range(NC8):
                k.op("vector", lambda e, c=c: e.tensor_tensor(
                    t16[:], outq[:, c * 512:(c + 1) * 512], PS[6][:], ALU.subtract),
                    waits=(ev_mb,), inc=V)
                ev_t26l = k.op("vector", lambda e: e.tensor_tensor(
                    t26[:], t16[:], PS[5][:], ALU.mult), waits=(ev_rb,), inc=V)
                ev_sts[c] = k.op("vector", lambda e, c=c: e.tensor_tensor(
                    xq[:, c * 512:(c + 1) * 512], xq[:, c * 512:(c + 1) * 512], t26[:],
                    ALU.add), waits=(ev_xq,), inc=V)
            prev_fin_p6 = (ev_m, ev_pf, ev_t26l)
            prev_app6 = (ev_t26l,)
            ev_out_final = k.dma(
                "sync",
                out_t.ap()[seq].rearrange("(c p) t -> p c t", p=128)[:, :, qc * 512:(qc + 1) * 512],
                xq[:].rearrange("p (c t) -> p c t", c=NC8), SD, waits=tuple(ev_sts))
        if seq + 1 < NSEQ:
            ev_x_pend[seq + 1] = [
                k.dma("sync", xres[:, c * S:(c + 1) * S],
                      x_t.ap()[seq + 1, c * 128:(c + 1) * 128, :], Y)
                for c in range(NC8)]
        dump("outq", outq[:], (128, NC8 * 512), BF16, waits=(prev_oq[0], prev_oq[1]))
        dump("st6", stat[:], (1, 1536), F32, waits=(prev_fin_p6[0], prev_fin_p6[1]))
        dump("xq6", xq[:], (128, NC8 * 512), F32, waits=(ev_out_final,))
        k.op("sync", lambda e: e.nop(), waits=(ev_out_final,))
        k.flush(f"p6_{seq}")

    allsems = [sm for pl in k.pools.values() for sm in pl["sems"]]
    k.op("scalar", lambda e: e.nop(),
         waits=tuple((sm, k.cnt[sm]) for sm in allsems))
    k.flush("fin")
    return nc, dbg_outs


def host_prep(inputs):
    g = lambda n: np.asarray(inputs[n], dtype=np.float32)
    x = g("L0_attn_x")
    common = {
        "wqkT": np.ascontiguousarray(g("L0_attn_Wqk")),
        "bqk": np.ascontiguousarray(g("L0_attn_bqk")),
        "gam": np.ascontiguousarray(g("L0_attn_gamma")),
        "bet": np.ascontiguousarray(g("L0_attn_beta")),
        "wvT": np.ascontiguousarray(g("L0_attn_Wv")),
        "bv": np.ascontiguousarray(g("L0_attn_bv")),
        "rel": np.ascontiguousarray(g("L0_attn_rel_emb")),
        "w_inT": np.ascontiguousarray(g("L0_ffn_w_in").T),
        "w_outT": np.ascontiguousarray(g("L0_ffn_w_out").T),
        "kT": np.ascontiguousarray(g("L0_ffn_k_ffwd").transpose(1, 0, 3, 2)),
        "vT": np.ascontiguousarray(g("L0_ffn_v_ffwd").transpose(1, 0, 2, 3)),
    }
    common.update(host_constants())
    in_maps = []
    for c in range(NCORES):
        m = dict(common)
        m["x_t"] = np.ascontiguousarray(x[2 * c:2 * c + 2].transpose(0, 2, 1))
        in_maps.append(m)
    return in_maps


def kernel(**inputs):
    nc, _ = build_program()
    in_maps = host_prep(inputs)
    res = run_bass_kernel_spmd(nc, in_maps, core_ids=list(range(NCORES)))
    out = np.empty((16, S, D), dtype=np.float32)
    for c in range(NCORES):
        out[2 * c:2 * c + 2] = res.results[c]["out_t"].transpose(0, 2, 1)
    return out


# version 31 (final)
# speedup vs baseline: 1.0211x; 1.0211x over previous
import math
import numpy as np
import ml_dtypes
import concourse.bass as bass
import concourse.mybir as mybir
from concourse.bass_utils import run_bass_kernel_spmd

F32 = mybir.dt.float32
BF16 = mybir.dt.bfloat16
AF = mybir.ActivationFunctionType
ALU = mybir.AluOpType

NCORES = 8
NSEQ = 2
S = 2048
D = 1024
NH = 8
DH = 128
NE = 16
SE = 256
DQK = 128
NC8 = D // 128
NTC = 4
NTILE = S // 128

MU = math.sqrt(0.5)
STD = math.sqrt(0.25 * math.pi)
ALPHA2 = 1.0 / (S * STD)
BIG_SCALE = (DQK ** 0.5) / STD
BIG_SHIFT = -MU / STD
NEG = -30000.0
EPS_Z = 1e-18
LN_EPS = 1e-5

ENGS = ("tensor", "vector", "scalar", "gpsimd", "sync")
KB = 1024


def bucket_table():
    n = np.arange(S)
    max_exact = 16
    is_small = n < max_exact
    nf = np.maximum(n, 1).astype(np.float32)
    val = max_exact + (np.log(nf / max_exact) / math.log(128 / max_exact) * 16).astype(np.int32)
    val = np.minimum(val, 31)
    return np.where(is_small, n, val)


def host_constants():
    bt = bucket_table()
    L = S + 511
    oh = np.zeros((32, L), dtype=np.float32)
    msk = np.zeros((1, L), dtype=np.float32)
    d = np.arange(L) - 511
    msk[0, d < 0] = NEG
    valid = np.where(d >= 0)[0]
    oh[bt[d[valid]], valid] = 1.0
    import jax
    import jax.numpy as jnp
    half = DH // 2
    with jax.default_device(jax.devices("cpu")[0]):
        inv_freq_j = 1.0 / 10000.0 ** (jnp.arange(half, dtype=jnp.float32) / half)
        ang_j = jnp.arange(S, dtype=jnp.float32)[:, None] * inv_freq_j
        cos = np.asarray(jnp.cos(ang_j), dtype=np.float32)
        sin = np.asarray(jnp.sin(ang_j), dtype=np.float32)
    cosT = np.concatenate([cos.T, cos.T], axis=0)
    sinpm = np.concatenate([-sin.T, sin.T], axis=0)
    bf = ml_dtypes.bfloat16
    tri = np.zeros((128, 4 * 512), dtype=np.float32)
    cc = np.arange(128)[:, None]
    jj = np.arange(512)[None, :]
    for r in range(4):
        tri[:, r * 512:(r + 1) * 512] = (cc + 128 * r <= jj)
    return {
        "c_oh": oh,
        "c_msk": msk,
        "c_cosk": cosT.astype(np.float32),
        "c_sink": sinpm.astype(np.float32),
        "c_ones": np.ones((128, 128), dtype=bf),
        "c_idn": np.eye(128, dtype=np.float32).astype(bf),
        "c_adn": np.eye(128, dtype=np.float32)[::-1].copy().astype(bf),
        "c_tri": tri.astype(bf),
    }


class K:

    def __init__(self, nc):
        self.nc = nc
        self.q = {e: [] for e in ENGS}
        self.sems = {}
        self.cnt = {}
        self.dma16 = set()
        self.hwm = {e: {} for e in ENGS}
        self._keep = []

    def sem(self, name, dma=False):
        cm = self.nc.semaphore(name)
        h = cm.__enter__()
        self._keep.append(cm)
        self.sems[name] = h
        self.cnt[name] = 0
        if dma:
            self.dma16.add(name)
        return name

    def op(self, eng, fn, waits=(), inc=None):
        flat = []
        for w in waits:
            if w is None:
                continue
            if isinstance(w, tuple) and len(w) == 2 and isinstance(w[0], str):
                flat.append(w)
            else:
                for ww in w:
                    if ww is not None:
                        flat.append(ww)
        ev = None
        if inc is not None:
            self.cnt[inc] += 16 if inc in self.dma16 else 1
            ev = (inc, self.cnt[inc])
        self.q[eng].append((fn, tuple(flat), inc))
        return ev

    def mkpool(self, name, n=6):
        self.pools = getattr(self, "pools", {})
        self.pools[name] = {"sems": [self.sem(f"{name}{i}", dma=True) for i in range(n)],
                            "i": 0}

    def dma(self, qeng, out_ap, in_ap, pool, waits=(), slow=False):
        p = self.pools[pool]
        sem = p["sems"][p["i"] % len(p["sems"])]
        p["i"] += 1
        w = tuple(waits) + ((sem, self.cnt[sem]),)
        if slow:
            return self.op(qeng, lambda e, o=out_ap, i=in_ap: e.dma_start(
                o, i, allow_slow_non_contiguous=True), waits=w, inc=sem)
        return self.op(qeng, lambda e, o=out_ap, i=in_ap: e.dma_start(o, i),
                       waits=w, inc=sem)

    def flush(self, name=None):
        nc = self.nc
        if not any(self.q[e] for e in ENGS):
            return
        with nc.Block(name) as blk:
            for eng in ENGS:
                items = self.q[eng]
                if not items:
                    continue

                def body(engine, items=items, eng=eng):
                    hw = self.hwm[eng]
                    for fn, waits, inc in items:
                        for (s, v) in waits:
                            if hw.get(s, 0) < v:
                                engine.wait_ge(self.sems[s], v)
                                hw[s] = v
                        inst = fn(engine)
                        if inc is not None:
                            inst.then_inc(self.sems[inc], 16 if inc in self.dma16 else 1)
                getattr(blk, eng)(body)
        self.q = {e: [] for e in ENGS}
        for s in self.cnt:
            if s in self.dma16:
                mw = max((self.hwm[e].get(s, 0) for e in ENGS), default=0)
            else:
                mw = self.cnt[s]
            for e in ENGS:
                self.hwm[e][s] = max(self.hwm[e].get(s, 0), mw)


def build_program(dbg=(), upto=None):
    nc = bass.Bass("TRN2", target_bir_lowering=False, detect_race_conditions=False)
    k = K(nc)
    _n = [0]

    SB_BASE = 17408

    def sb(off_kb, name, shape, dtype):
        _n[0] += 1
        off = int(round(off_kb * KB)) + SB_BASE
        assert off % 64 == 0, (name, off_kb)
        assert off + shape[-1] * (2 if dtype == BF16 else 4) <= 229247, (name, off_kb)
        return nc.alloc_sbuf_tensor_at(f"{name}{_n[0]}", shape, dtype, offset=off)

    DI = lambda n, s, d: nc.dram_tensor(n, s, d, kind="ExternalInput")
    x_t = DI("x_t", [NSEQ, D, S], F32)
    wqkT_d = DI("wqkT", [D, DQK], F32)
    bqk_d = DI("bqk", [DQK], F32)
    gam_d = DI("gam", [2, DQK], F32)
    bet_d = DI("bet", [2, DQK], F32)
    wvT_d = DI("wvT", [D, D], F32)
    bv_d = DI("bv", [D], F32)
    rel_d = DI("rel", [32, 1], F32)
    w_inT_d = DI("w_inT", [D, 4112], F32)
    w_outT_d = DI("w_outT", [2 * D, D], F32)
    kT_d = DI("kT", [NH, NE, DH, SE], F32)
    vT_d = DI("vT", [NH, NE, SE, DH], F32)
    c_oh = DI("c_oh", [32, S + 511], F32)
    c_msk = DI("c_msk", [1, S + 511], F32)
    c_cosk = DI("c_cosk", [128, S], F32)
    c_sink = DI("c_sink", [128, S], F32)
    c_ones = DI("c_ones", [128, 128], BF16)
    c_idn = DI("c_idn", [128, 128], BF16)
    c_adn = DI("c_adn", [128, 128], BF16)
    c_tri = DI("c_tri", [128, 4 * 512], BF16)

    out_t = nc.dram_tensor("out_t", [NSEQ, D, S], F32, kind="ExternalOutput")

    fb_d = nc.dram_tensor("fb_d", [S + 511], BF16, kind="Internal")
    xsp_d = nc.dram_tensor("xsp_d", [128, NC8 * S], F32, kind="ExternalOutput")
    xn_d = nc.dram_tensor("xn_d", [128, NC8 * S], BF16, kind="ExternalOutput")
    g_d = nc.dram_tensor("g_d", [NE, S], BF16, kind="ExternalOutput")
    bc_d = nc.dram_tensor("bc_d", [8, S], F32, kind="ExternalOutput")
    rc_d = nc.dram_tensor("rc_d", [8, 512], F32, kind="ExternalOutput")
    rl_d = nc.dram_tensor("rl_d", [128, 256], F32, kind="ExternalOutput")

    dbg_outs = {}

    def dump(name, ap, shape, dtype, waits=()):
        if name not in dbg:
            return
        n = 2
        nm = "dbg_" + name
        while nm in dbg_outs.values():
            nm = f"dbg_{name}__{n}"
            n += 1
        dd = nc.dram_tensor(nm, list(shape), dtype, kind="ExternalOutput")
        dbg_outs[nm] = nm
        k.dma("sync", dd.ap(), ap, "Y", waits=waits)

    k.mkpool("Y", 8)
    k.mkpool("SD", 6)
    k.mkpool("GD", 8)
    Y, SD, GD = "Y", "SD", "GD"
    T = k.sem("T")
    V = k.sem("V")
    SC = k.sem("SC")
    G = k.sem("G")

    P0 = 192.0
    ones = sb(P0 + 0.000, "ones", [128, 128], BF16)
    idn = sb(P0 + 0.250, "idn", [128, 128], BF16)
    recs = sb(P0 + 0.500, "recs", [128, 512], F32)
    bqk_s = sb(P0 + 2.500, "bqk_s", [128, 1], F32)
    gb4 = sb(P0 + 2.625, "gb4", [128, 4], F32)
    bv_s = sb(P0 + 2.750, "bv_s", [1, D], BF16)
    rel_s = sb(P0 + 4.750, "rel_s", [32, 1], F32)
    gbraw = sb(P0 + 4.875, "gbraw", [128, 4], F32)
    stat = sb(P0 + 5.000, "stat", [1, 1536], F32)
    cst = sb(P0 + 11.000, "cst", [128, 4], F32)
    onesr = sb(P0 + 11.250, "onesr", [1, 128], F32)
    adn = sb(P0 + 11.750, "adn", [128, 128], BF16)
    colw = sb(P0 + 11.125, "colw", [1, 16], F32)

    PS = [nc.alloc_psum_tensor(f"ps{i}", [128, 512], F32) for i in range(7)]
    ps_tr = nc.alloc_psum_tensor("ps_tr", [128, 256], BF16)

    xres = sb(0, "xres", [128, NC8 * S], F32)
    v_tm = sb(64, "v_tm", [128, NTILE * D], BF16)
    attnT = sb(96, "attnT", [128, 16 * 512], BF16)
    attnT2 = sb(137.5, "attnT2", [128, 16 * 512], BF16)
    q_s = sb(112, "q_s", [128, S], BF16)
    k_s = sb(116, "k_s", [128, S], BF16)
    big = sb(120, "big", [128, S + 384], F32)
    xn1 = sb(129.5, "xn1", [128, NC8 * 512], BF16)
    bigrev = sb(129.5, "bigrev", [128, S + 384], BF16)
    wv_s = sb(137.5, "wv_s", [128, NC8 * D], BF16)
    wqk_s = sb(153.5, "wqk_s", [128, D], BF16)
    zeps = sb(155.5, "zeps", [128, 2 * 512], F32)
    gel = sb(159.5, "gel", [128, 2 * 512], F32)
    qksc = sb(163.5, "qksc", [128, 512], F32)
    sq32 = sb(163.5, "sq32", [128, 512], F32)
    ao1q = sb(165.5, "ao1q", [128, NC8 * 512], F32)
    xbfs = sb(165.5, "xbfs", [128, 2048], BF16)
    sqs = sb(169.5, "sqs", [128, 2048], BF16)
    mb = sb(181.5, "mb", [128, 512], F32)
    rb = sb(183.5, "rb", [128, 512], F32)
    t1s = sb(185.5, "t1s", [128, 512], F32)
    t2s = sb(187.5, "t2s", [128, 512], F32)
    wr32 = sb(189.5, "wr32", [128, NC8 * 16], F32)

    ffwd = sb(64, "ffwd", [128, NH * S], BF16)
    qf = sb(32, "qf", [128, NH * S], BF16)
    kh = sb(96, "kh", [128, 2 * NE * SE], BF16)
    vh = sb(112, "vh", [128, 2 * NE * 256], BF16)
    xn4 = sb(128, "xn4", [128, 2 * NC8 * 512], BF16)
    a_sb = sb(144, "a_sb", [128, 2 * 2 * S], BF16)
    g_b = sb(160, "g_b", [128, 2 * S], BF16)
    wstr = sb(168, "wstr", [128, NC8 * D], BF16)
    rlog = sb(184, "rlog", [128, 256], F32)
    r4 = sb(185, "r4", [128, 4], F32)
    ttmp = sb(185.25, "ttmp", [128, 16], F32)
    stmp2 = sb(185.5, "stmp2", [128, 16], F32)
    gtm = sb(186, "gtm", [128, 256], BF16)
    gfm = sb(186.5, "gfm", [16, S], BF16)
    qa = sb(32, "qa", [128, NH * S], BF16)
    ka = sb(128, "ka", [128, NH * S], BF16)
    va = sb(160, "va", [128, NTILE * D], BF16)
    xn5 = sb(0, "xn5", [128, 2 * NC8 * 512], BF16)
    wka = sb(16, "wka", [128, NH * NC8 * 128], BF16)
    tbl_c = sb(96, "tbl_c", [128, S], BF16)
    tbl_s = sb(100, "tbl_s", [128, S], BF16)
    kraw = sb(104, "kraw", [128, 2 * 512], BF16)
    ksw = sb(106, "ksw", [128, 2 * 512], BF16)
    xs_r = sb(108, "xs_r", [128, 2 * 512], BF16)
    wva = sb(110, "wva", [128, NC8 * D], BF16)
    ao2 = sb(96, "ao2", [128, NH * S], BF16)
    tri_s = sb(0, "tri_s", [128, 4 * 512], BF16)
    expT = sb(4, "expT", [128, 4 * 512], BF16)
    rcb = sb(8, "rcb", [128, 4 * 512], F32)
    wo_s = sb(128, "wo_s", [128, NC8 * 16 * 128], BF16)
    xq = sb(160, "xq", [128, NC8 * 512], F32)
    outq = sb(176, "outq", [128, NC8 * 512], BF16)
    t16 = sb(184, "t16", [128, 512], F32)
    t26 = sb(186, "t26", [128, 512], F32)
    sq6 = sb(188, "sq6", [128, 512], BF16)


    tmp_oh = sb(0, "tmp_oh", [32, S + 511], F32)
    tmp_msk = sb(10.5, "tmp_msk", [1, S + 511], F32)
    tmp_fb = sb(21, "tmp_fb", [1, S + 511], F32)

    k.dma("sync", ones[:], c_ones.ap(), Y)
    k.dma("sync", idn[:], c_idn.ap(), Y)
    k.dma("sync", adn[:], c_adn.ap(), Y)
    k.dma("sync", bqk_s[:], bqk_d.ap().rearrange("(p o) -> p o", o=1), Y)
    k.dma("sync", gbraw[:, 0:1], gam_d.ap()[0:1, :].rearrange("o p -> p o"), Y)
    k.dma("sync", gbraw[:, 1:2], bet_d.ap()[0:1, :].rearrange("o p -> p o"), Y)
    k.dma("sync", gbraw[:, 2:3], gam_d.ap()[1:2, :].rearrange("o p -> p o"), Y)
    ev_gbl = k.dma("sync", gbraw[:, 3:4], bet_d.ap()[1:2, :].rearrange("o p -> p o"), Y)
    k.dma("sync", rel_s[:], rel_d.ap(), Y)
    k.dma("sync", tmp_oh[:], c_oh.ap(), Y)
    ev_msk = k.dma("sync", tmp_msk[:], c_msk.ap(), Y)
    k.dma("gpsimd", bv_s[:], bv_d.ap().rearrange("(o n) -> o n", o=1), GD)
    k.op("gpsimd", lambda e: e.memset(cst[:, 0:1], float(LN_EPS)))
    k.op("gpsimd", lambda e: e.memset(cst[:, 1:2], 1.0))
    k.op("gpsimd", lambda e: e.memset(cst[:, 2:3], float(1.0 / math.sqrt(DH))))
    k.op("gpsimd", lambda e: e.memset(onesr[:], 1.0))
    nc.const_aps.register(F32, float(LN_EPS), cst[:, 0:1])
    nc.const_aps.register(F32, float(1.0 / math.sqrt(DH)), cst[:, 2:3])

    ev_gb = k.op("vector", lambda e: e.tensor_scalar(
        gb4[:, 0:2], gbraw[:, 0:2], float(ALPHA2), None, ALU.mult),
        waits=(ev_gbl,), inc=V)
    k.op("vector", lambda e: e.tensor_copy(gb4[:, 2:4], gbraw[:, 2:4]),
         waits=(ev_gb,), inc=V)

    nfb = S + 511
    ev_fparts = []
    prev = None
    for ci in range((nfb + 511) // 512):
        c0 = ci * 512
        c1 = min(nfb, c0 + 512)
        ps = PS[ci % 2]
        evm = k.op("tensor", lambda e, ps=ps, c0=c0, c1=c1: e.matmul(
            ps[0:1, 0:c1 - c0], rel_s[:], tmp_oh[:, c0:c1], start=True, stop=True),
            waits=(ev_msk, prev), inc=T)
        evs = k.op("scalar", lambda e, ps=ps, c0=c0, c1=c1: e.activation(
            tmp_fb[0:1, c0:c1], ps[0:1, 0:c1 - c0], AF.Copy,
            bias=float(BIG_SHIFT), scale=float(BIG_SCALE)), waits=(evm,), inc=SC)
        prev = evs
        ev_fparts.append(evs)
    ev_fb = k.op("vector", lambda e: e.tensor_tensor(
        tmp_fb[:], tmp_fb[:], tmp_msk[:], ALU.add), waits=tuple(ev_fparts), inc=V)
    ev_fbd = k.dma("gpsimd", fb_d.ap(), tmp_fb[:], GD, waits=(ev_fb,))
    ev_x_pend = [None, None]
    ev_x_pend[0] = [k.dma("sync", xres[:, c * S:(c + 1) * S],
                          x_t.ap()[0, c * 128:(c + 1) * 128, :], Y,
                          waits=(ev_fbd,))
                    for c in range(NC8)]
    k.flush("setup")
    if upto == "setup":
        return nc, dbg_outs

    stage_guard = [None, None, None, None]

    def stat_finalize(ps, row_m, row_sq, waits, ps_sq=None):
        if ps_sq is None:
            ps_sq = ps
        w = tuple(waits) + tuple(g for g in stage_guard if g is not None)
        e1 = k.op("vector", lambda e: e.tensor_scalar(
            stat[0:1, 0:512], ps[row_m:row_m + 1, :], 1.0 / D, None, ALU.mult),
            waits=w, inc=V)
        e2 = k.op("vector", lambda e: e.tensor_scalar(
            stat[0:1, 512:1024], ps_sq[row_sq:row_sq + 1, :], 1.0 / D, None, ALU.mult),
            waits=w, inc=V)
        e3 = k.op("vector", lambda e: e.tensor_tensor(
            stat[0:1, 1024:1536], stat[0:1, 0:512], stat[0:1, 0:512], ALU.mult),
            waits=(e1,), inc=V)
        e4 = k.op("vector", lambda e: e.tensor_tensor(
            stat[0:1, 512:1024], stat[0:1, 512:1024], stat[0:1, 1024:1536],
            ALU.subtract), waits=(e2, e3), inc=V)
        e5a = k.op("scalar", lambda e: e.activation(
            stat[0:1, 1024:1536], stat[0:1, 512:1024], AF.Sqrt, bias=float(LN_EPS)),
            waits=(e4,), inc=SC)
        e5 = k.op("vector", lambda e: e.reciprocal(
            stat[0:1, 512:1024], stat[0:1, 1024:1536]), waits=(e5a,), inc=V)
        return e1, e5, e2

    def bcast_psum(ps_m, ps_r, ev_m, ev_r, free_m=(), free_r=()):
        em = k.op("tensor", lambda e: e.matmul(
            ps_m[:], onesr[:], stat[0:1, 0:512], start=True, stop=True),
            waits=(ev_m,) + tuple(free_m), inc=T)
        er = k.op("tensor", lambda e: e.matmul(
            ps_r[:], onesr[:], stat[0:1, 512:1024], start=True, stop=True),
            waits=(ev_r,) + tuple(free_r), inc=T)
        stage_guard[0] = em
        stage_guard[1] = er
        return em, er

    def stage_bcast(row_md, row_rd, col0, mb_ap, rb_ap, ev_m, ev_r, free_evs=()):
        sl = slice(col0, col0 + 512)
        w1 = k.dma("sync", bc_d.ap()[row_md:row_md + 1, sl], stat[0:1, 0:512], SD,
                   waits=(ev_m,))
        w2 = k.dma("sync", bc_d.ap()[row_rd:row_rd + 1, sl], stat[0:1, 512:1024], SD,
                   waits=(ev_r,))
        stage_guard[0] = w1
        stage_guard[1] = w2
        stage_guard[2] = None
        stage_guard[3] = None
        b1 = k.dma("sync", mb_ap,
                   bc_d.ap()[row_md:row_md + 1, sl].to_broadcast((128, 512)), Y,
                   waits=(w1,) + tuple(free_evs))
        b2 = k.dma("sync", rb_ap,
                   bc_d.ap()[row_rd:row_rd + 1, sl].to_broadcast((128, 512)), Y,
                   waits=(w2,) + tuple(free_evs))
        return b1, b2

    ev_xnw = [[None] * NTC for _ in range(NSEQ)]
    ev_xsw = [[None] * NTC for _ in range(NSEQ)]

    for seq in range(NSEQ):
        if ev_x_pend[seq] is None:
            ev_x_pend[seq] = [k.dma("sync", xres[:, c * S:(c + 1) * S],
                                    x_t.ap()[seq, c * 128:(c + 1) * 128, :], Y)
                              for c in range(NC8)]
        ev_x = ev_x_pend[seq]
        ev_brev = k.dma("sync", bigrev[:], bass.AP(fb_d, 0, [[1, 128], [1, S + 384]]),
                        Y, waits=(ev_fbd, ev_x[0]))
        ev_big = None
        prev_bg = None
        for cb in range(5):
            c0 = cb * 512
            c1 = min(S + 384, c0 + 512)
            psb = PS[cb % 2]
            eb = k.op("tensor", lambda e, psb=psb, c0=c0, c1=c1: e.matmul(
                psb[:, 0:c1 - c0], adn[:], bigrev[:, c0:c1], start=True, stop=True),
                waits=(ev_brev, prev_bg), inc=T)
            ev_big = k.op("vector", lambda e, psb=psb, c0=c0, c1=c1: e.tensor_copy(
                big[:, c0:c1], psb[:, 0:c1 - c0]), waits=(eb,), inc=V)
            prev_bg = ev_big
        ev_wqk = k.dma("gpsimd", wqk_s[:].rearrange("p (c o) -> p c o", c=NC8),
                       wqkT_d.ap().rearrange("(c p) o -> p c o", p=128), GD)
        ev_wv = [None] * NC8
        for c in range(NC8):
            ev_wv[c] = k.dma("gpsimd", wv_s[:, c * D:(c + 1) * D],
                             wvT_d.ap()[c * 128:(c + 1) * 128, :], GD)

        ev_wr32 = k.dma("sync", wr32[:].rearrange("p (c o) -> p c o", c=NC8),
                        w_inT_d.ap()[:, 4096:4112].rearrange("(c p) o -> p c o", p=128), Y)
        ev_cw = None
        for c in range(NC8):
            ev_cw = k.op("tensor", lambda e, c=c: e.matmul(
                PS[6][0:1, 0:16], cst[:, 1:2], wr32[:, c * 16:(c + 1) * 16],
                start=(c == 0), stop=(c == NC8 - 1)), waits=(ev_wr32,), inc=T)
        ev_cw = k.op("vector", lambda e: e.tensor_copy(
            colw[:], PS[6][0:1, 0:16]), waits=(ev_cw,), inc=V)

        ev_q = [None] * NTC
        ev_k = [None] * NTC
        ev_vevac = [[None, None] for _ in range(NTILE)]
        prev_qk_evac = [ev_big, ev_big]
        prev_v_evac = [None, None]
        mbrb_free = ()
        last_xn1 = None
        prev_fin1 = ()
        cast_gate = [None] * 4
        for tc in range(NTC):
            ev_st = None
            for c in range(NC8):
                sl = slice((c % 4) * 512, (c % 4) * 512 + 512)
                ev_cast = k.dma("gpsimd", xbfs[:, sl],
                                x_t.ap()[seq, c * 128:(c + 1) * 128, tc * 512:(tc + 1) * 512],
                                GD, waits=(cast_gate[c % 4],))
                ev_sq = k.op("vector", lambda e, sl=sl: e.tensor_tensor(
                    sqs[:, sl], xbfs[:, sl], xbfs[:, sl], ALU.mult),
                    waits=(ev_cast,), inc=V)
                k.op("tensor", lambda e, c=c, sl=sl: e.matmul(
                    PS[4][0:1, :], ones[:, 0:1], xbfs[:, sl],
                    start=(c == 0), stop=(c == NC8 - 1)),
                    waits=(ev_cast,) + prev_fin1, inc=T)
                ev_st = k.op("tensor", lambda e, c=c, sl=sl: e.matmul(
                    PS[4][32:33, :], ones[:, 0:1], sqs[:, sl],
                    start=(c == 0), stop=(c == NC8 - 1)),
                    waits=(ev_sq,) + prev_fin1, inc=T)
                cast_gate[c % 4] = ev_st
            ev_m, ev_r, ev_pf = stat_finalize(PS[4], 0, 32, waits=(ev_st,))
            prev_fin1 = (ev_m, ev_pf)
            ev_mb, ev_rb = bcast_psum(PS[5], PS[6], ev_m, ev_r,
                                      free_m=mbrb_free, free_r=mbrb_free + ((ev_cw,) if tc == 0 else ()))
            ev_xn = [None] * NC8
            for c in range(NC8):
                e1 = k.op("vector", lambda e, c=c, tc=tc: e.tensor_tensor(
                    t1s[:], xres[:, c * S + tc * 512: c * S + tc * 512 + 512], PS[5][:],
                    ALU.subtract), waits=(ev_mb, ev_x[c]), inc=V)
                ev_xn[c] = k.op("vector", lambda e, c=c: e.tensor_tensor(
                    xn1[:, c * 512:(c + 1) * 512], t1s[:], PS[6][:], ALU.mult),
                    waits=(ev_rb, e1, ev_big if tc == 0 else None), inc=V)
            mbrb_free = (ev_xn[NC8 - 1],)
            last_xn1 = ev_xn[NC8 - 1]
            psq = PS[tc % 2]
            evq = None
            for c in range(NC8):
                evq = k.op("tensor", lambda e, psq=psq, c=c: e.matmul(
                    psq[:], wqk_s[:, c * 128:(c + 1) * 128], xn1[:, c * 512:(c + 1) * 512],
                    start=(c == 0), stop=(c == NC8 - 1)),
                    waits=(ev_xn[c], ev_wqk, prev_qk_evac[tc % 2]), inc=T)
            ev_silu = k.op("scalar", lambda e, psq=psq: e.activation(
                qksc[:], psq[:], AF.Silu, bias=bqk_s[:, 0:1]),
                waits=(evq, ev_k[tc - 1] if tc > 0 else None), inc=SC)
            prev_qk_evac[tc % 2] = ev_silu
            ev_q[tc] = k.op("vector", lambda e, tc=tc: e.tensor_scalar(
                q_s[:, tc * 512:(tc + 1) * 512], qksc[:], gb4[:, 0:1], gb4[:, 1:2],
                ALU.mult, ALU.add), waits=(ev_silu,), inc=V)
            ev_k[tc] = k.op("vector", lambda e, tc=tc: e.tensor_scalar(
                k_s[:, tc * 512:(tc + 1) * 512], qksc[:], gb4[:, 2:3], gb4[:, 3:4],
                ALU.mult, ALU.add), waits=(ev_silu,), inc=V)
            for jji in range(4):
                j = tc * 4 + jji
                for nh in range(2):
                    psv = PS[2 + nh]
                    evv = None
                    for c in range(NC8):
                        evv = k.op("tensor", lambda e, psv=psv, c=c, jji=jji, nh=nh: e.matmul(
                            psv[:], xn1[:, c * 512 + jji * 128: c * 512 + jji * 128 + 128],
                            wv_s[:, c * D + nh * 512: c * D + nh * 512 + 512],
                            start=(c == 0), stop=False),
                            waits=(ev_xn[c], ev_wv[c], prev_v_evac[nh]), inc=T)
                    evv = k.op("tensor", lambda e, psv=psv, nh=nh: e.matmul(
                        psv[:], ones[0:1, 0:128], bv_s[0:1, nh * 512:nh * 512 + 512],
                        start=False, stop=True), waits=(evv,), inc=T)
                    pv = k.op("scalar", lambda e, psv=psv, j=j, nh=nh: e.activation(
                        v_tm[:, j * D + nh * 512: j * D + nh * 512 + 512], psv[:], AF.Silu),
                        waits=(evv,), inc=SC)
                    prev_v_evac[nh] = pv
                    ev_vevac[j][nh] = pv
        dump("xn1", xn1[:], (128, NC8 * 512), BF16, waits=(last_xn1,))
        dump("q_s", q_s[:], (128, S), BF16, waits=(ev_q[3],))
        dump("k_s", k_s[:], (128, S), BF16, waits=(ev_k[3],))
        dump("v_tm", v_tm[:], (128, NTILE * D), BF16, waits=(ev_vevac[15][1],))
        k.flush(f"h1a_{seq}")
        if upto == "h1a" and seq == 0:
            return nc, dbg_outs

        ev_at_b = [[None] * 16, [None] * 16]
        prev_z = [None, None]
        prev_div = [None, None]
        ev_rlw = [None] * NTC
        prev_router_ev = None
        ev_applydone = None
        ev_aoe_all_prev = ()
        prev_fin_p2 = ()
        prev_fin_p3 = ()
        h1b_state = {"mbrb_free": mbrb_free, "prev_fin_p2": (), "prev_fin_p3": (),
                     "prev_router_ev": None, "ev_applydone": None,
                     "ev_aoe_all_prev": (), "prev_ao": [None, None]}

        def emit_simtile(qc, b, aoe_gate):
            at_dst = attnT if qc % 2 == 0 else attnT2
            psm = PS[b % 2]
            ev_sim = k.op("tensor", lambda e, psm=psm, b=b, qc=qc: e.matmul(
                psm[:], k_s[:, b * 128:(b + 1) * 128],
                q_s[:, qc * 512:(qc + 1) * 512], start=True, stop=True),
                waits=(ev_k[b // 4], ev_q[qc], prev_z[b % 2]), inc=T)
            off = qc * 512 - b * 128 + 384
            ev_z = k.op("vector", lambda e, psm=psm, b=b, off=off: e.scalar_tensor_tensor(
                zeps[:, (b % 2) * 512:(b % 2) * 512 + 512], psm[:], EPS_Z,
                big[:, off:off + 512], ALU.add, ALU.add),
                waits=(ev_sim, ev_big, prev_div[b % 2]), inc=V)
            prev_z[b % 2] = ev_z
            ev_g = k.op("scalar", lambda e, b=b: e.activation(
                gel[:, (b % 2) * 512:(b % 2) * 512 + 512],
                zeps[:, (b % 2) * 512:(b % 2) * 512 + 512], AF.Gelu),
                waits=(ev_z, prev_div[b % 2]), inc=SC)
            ev_rc1 = k.op("vector", lambda e, b=b: e.reciprocal(
                zeps[:, (b % 2) * 512:(b % 2) * 512 + 512],
                zeps[:, (b % 2) * 512:(b % 2) * 512 + 512]),
                waits=(ev_g,), inc=V)
            ev_at_b[qc % 2][b] = k.op("gpsimd", lambda e, at_dst=at_dst, b=b: e.tensor_tensor(
                at_dst[:, b * 512:(b + 1) * 512],
                gel[:, (b % 2) * 512:(b % 2) * 512 + 512],
                zeps[:, (b % 2) * 512:(b % 2) * 512 + 512], ALU.mult),
                waits=(ev_g, ev_rc1) + tuple(aoe_gate), inc=G)
            prev_div[b % 2] = ev_at_b[qc % 2][b]

        def emit_av(qc, aoe_gate_for_sim, router_pend):
            st = h1b_state
            nb = 4 * qc + 4
            at_src = attnT if qc % 2 == 0 else attnT2
            ev_src = ev_at_b[qc % 2]
            nb2 = 4 * (qc + 1) + 4 if qc < NTC - 1 else 0
            done = 0
            ev_aoe = [None] * NC8
            prev_ao = st["prev_ao"]
            ev_p2 = None
            ev_sq = None

            def emit_stats(dv):
                nonlocal ev_p2, ev_sq
                ev_sq = k.op("vector", lambda e, dv=dv: e.tensor_tensor(
                    sq32[:], ao1q[:, dv * 512:(dv + 1) * 512],
                    ao1q[:, dv * 512:(dv + 1) * 512], ALU.mult),
                    waits=(ev_aoe[dv], ev_p2, st["prev_router_ev"]), inc=V)
                k.op("tensor", lambda e, dv=dv: e.matmul(
                    PS[6][0:1, :], cst[:, 1:2], ao1q[:, dv * 512:(dv + 1) * 512],
                    start=(dv == 0), stop=(dv == NC8 - 1)),
                    waits=(ev_aoe[dv],) + tuple(st["prev_fin_p2"]), inc=T)
                ev_p2 = k.op("tensor", lambda e, dv=dv: e.matmul(
                    PS[6][32:33, :], cst[:, 1:2], sq32[:],
                    start=(dv == 0), stop=(dv == NC8 - 1)),
                    waits=(ev_sq,) + tuple(st["prev_fin_p2"]), inc=T)

            for dv in range(NC8):
                if dv == 3 and router_pend is not None:
                    emit_router(*router_pend)
                target = min(nb2, (nb2 * (dv + 1) + NC8 - 1) // NC8)
                while done < target:
                    emit_simtile(qc + 1, done, aoe_gate_for_sim)
                    done += 1
                psa = PS[2 + dv % 2]
                eva = None
                for b in range(nb):
                    eva = k.op("tensor", lambda e, psa=psa, at_src=at_src, b=b, dv=dv, nb=nb: e.matmul(
                        psa[:], v_tm[:, b * D + dv * 128: b * D + dv * 128 + 128],
                        at_src[:, b * 512:(b + 1) * 512],
                        start=(b == 0), stop=(b == nb - 1)),
                        waits=(ev_src[b], ev_vevac[b][dv // 4], prev_ao[dv % 2]), inc=T)
                ev_aoe[dv] = k.op("scalar", lambda e, psa=psa, dv=dv: e.activation(
                    ao1q[:, dv * 512:(dv + 1) * 512], psa[:], AF.Copy),
                    waits=(eva, h1b_state["ev_applydone"]), inc=SC)
                prev_ao[dv % 2] = ev_aoe[dv]
                if dv >= 4:
                    emit_stats(dv - 4)
            while done < nb2:
                emit_simtile(qc + 1, done, aoe_gate_for_sim)
                done += 1
            if router_pend is not None and NC8 <= 3:
                emit_router(*router_pend)
            for dv in range(4, NC8):
                emit_stats(dv)
            h1b_state["ev_aoe_all_prev"] = tuple(ev_aoe)
            return ev_aoe, ev_p2

        def emit_post(qc, ev_aoe, ev_p2):
            st = h1b_state
            ev_m, ev_r, ev_pf = stat_finalize(PS[6], 0, 32, waits=(ev_p2,))
            ev_mb, ev_rb = bcast_psum(PS[6], PS[5], ev_m, ev_r,
                                      free_m=(ev_pf,) + tuple(st["mbrb_free"]),
                                      free_r=tuple(st["mbrb_free"]) + tuple(st["prev_fin_p3"]))
            ev_xup = [None] * NC8
            ev_t2l = None
            for c in range(NC8):
                k.op("vector", lambda e, c=c: e.tensor_tensor(
                    t1s[:], ao1q[:, c * 512:(c + 1) * 512], PS[6][:], ALU.subtract),
                    waits=(ev_mb,), inc=V)
                ev_t2l = k.op("vector", lambda e: e.tensor_tensor(
                    t2s[:], t1s[:], PS[5][:], ALU.mult), waits=(ev_rb,), inc=V)
                ev_xup[c] = k.op("vector", lambda e, c=c, qc=qc: e.tensor_tensor(
                    xres[:, c * S + qc * 512: c * S + qc * 512 + 512],
                    xres[:, c * S + qc * 512: c * S + qc * 512 + 512], t2s[:], ALU.add),
                    inc=V)
            st["ev_applydone"] = ev_xup[NC8 - 1]
            st["mbrb_free"] = (ev_xup[NC8 - 1],)
            st["prev_fin_p2"] = (ev_m, ev_pf, ev_t2l)
            ev_p3 = None
            for c in range(NC8):
                ev_sq = k.op("vector", lambda e, c=c, qc=qc: e.tensor_tensor(
                    sq32[:], xres[:, c * S + qc * 512: c * S + qc * 512 + 512],
                    xres[:, c * S + qc * 512: c * S + qc * 512 + 512], ALU.mult),
                    waits=(ev_xup[c], ev_p3), inc=V)
                k.op("tensor", lambda e, c=c, qc=qc: e.matmul(
                    PS[5][0:1, :], cst[:, 1:2],
                    xres[:, c * S + qc * 512: c * S + qc * 512 + 512],
                    start=(c == 0), stop=(c == NC8 - 1)),
                    waits=(ev_xup[c], ev_t2l) + tuple(st["prev_fin_p3"]), inc=T)
                ev_p3 = k.op("tensor", lambda e, c=c: e.matmul(
                    PS[5][32:33, :], cst[:, 1:2], sq32[:],
                    start=(c == 0), stop=(c == NC8 - 1)),
                    waits=(ev_sq,) + tuple(st["prev_fin_p3"]), inc=T)
            ev_m2, ev_r2, ev_pf3 = stat_finalize(PS[5], 0, 32, waits=(ev_p3,))
            ev_mb2, ev_rb2 = bcast_psum(PS[5], PS[6], ev_m2, ev_r2,
                                        free_m=(ev_pf3,),
                                        free_r=(ev_t2l, ev_xup[NC8 - 1]))
            ev_xn2 = [None] * NC8
            for c in range(NC8):
                e1 = k.op("vector", lambda e, c=c, qc=qc: e.tensor_tensor(
                    t1s[:], xres[:, c * S + qc * 512: c * S + qc * 512 + 512], PS[5][:],
                    ALU.subtract), waits=(ev_mb2,), inc=V)
                ev_xn2[c] = k.op("vector", lambda e, c=c: e.tensor_tensor(
                    xn1[:, c * 512:(c + 1) * 512], t1s[:], PS[6][:], ALU.mult),
                    waits=(ev_rb2, e1), inc=V)
            st["mbrb_free"] = (ev_xn2[NC8 - 1],)
            st["prev_fin_p3"] = (ev_m2, ev_pf3, ev_xn2[NC8 - 1])
            ev_xnw[seq][qc] = k.dma(
                "sync",
                xn_d.ap().rearrange("p (c t) -> p c t", c=NC8)[:, :, qc * 512:(qc + 1) * 512],
                xn1[:].rearrange("p (c t) -> p c t", c=NC8), SD,
                waits=tuple(ev_xn2))
            ev_xsw[seq][qc] = k.dma(
                "sync",
                xsp_d.ap().rearrange("p (c t) -> p c t", c=NC8)[:, :, qc * 512:(qc + 1) * 512],
                xres[:].rearrange("p (c t) -> p c t", c=NC8)[:, :, qc * 512:(qc + 1) * 512],
                SD, waits=tuple(ev_xup))
            st["prev_fin_p2"] = st["prev_fin_p2"] + (ev_xn2[NC8 - 1],)
            return (qc, ev_m2, ev_r2, ev_xup)

        def emit_router(qc, ev_m2, ev_r2, ev_xup):
            st = h1b_state
            ev_ng = k.op("vector", lambda e: e.tensor_scalar(
                recs[0:1, 0:512], stat[0:1, 0:512], -1.0, None, ALU.mult),
                waits=(ev_m2,), inc=V)
            ev_rst = k.dma("sync", bc_d.ap()[5:6, qc * 512:(qc + 1) * 512],
                           stat[0:1, 512:1024], SD, waits=(ev_r2,))
            ev_rtm = k.dma("gpsimd", gbraw[:, 0:4],
                           bc_d.ap()[5:6, qc * 512:(qc + 1) * 512]
                           .rearrange("o (j p) -> p (o j)", p=128), GD,
                           waits=(ev_rst, st["prev_router_ev"]), slow=True)
            stage_guard[2] = ev_rst
            ev_rt_evac = st["prev_router_ev"]
            for jji in range(4):
                evr = None
                for c in range(NC8):
                    evr = k.op("tensor", lambda e, c=c, jji=jji, qc=qc: e.matmul(
                        PS[4][:, 0:16],
                        xres[:, c * S + qc * 512 + jji * 128: c * S + qc * 512 + jji * 128 + 128],
                        wr32[:, c * 16:(c + 1) * 16],
                        start=(c == 0), stop=False),
                        waits=(ev_xup[c], ev_cw, ev_rt_evac), inc=T)
                evr = k.op("tensor", lambda e, jji=jji: e.matmul(
                    PS[4][:, 0:16], recs[0:1, jji * 128:jji * 128 + 128],
                    colw[:], start=False, stop=True),
                    waits=(ev_ng,), inc=T)
                ev_rt_evac = k.op("scalar", lambda e, jji=jji: e.activation(
                    sq32[:, jji * 16:(jji + 1) * 16], PS[4][:, 0:16], AF.Copy,
                    scale=gbraw[:, jji:jji + 1]), waits=(evr, ev_rtm), inc=SC)
            ev_rlw[qc] = k.dma("sync", rl_d.ap()[:, qc * 64:(qc + 1) * 64],
                               sq32[:, 0:64], SD, waits=(ev_rt_evac,))
            st["prev_router_ev"] = ev_rlw[qc]
            stage_guard[3] = ev_ng

        for b0 in range(4):
            emit_simtile(0, b0, ())
        aoe_m1 = ()
        router_pend = None
        for qc in range(NTC):
            ev_aoe, ev_p2 = emit_av(qc, aoe_m1, router_pend)
            router_pend = emit_post(qc, ev_aoe, ev_p2)
            aoe_m1 = tuple(ev_aoe)
        emit_router(*router_pend)
        ev_aoe_all_prev = h1b_state["ev_aoe_all_prev"]
        k.op("sync", lambda e: e.nop(),
             waits=(ev_xsw[seq][NTC - 1], ev_xnw[seq][NTC - 1], ev_rlw[NTC - 1]))
        dump("ao1q", ao1q[:], (128, NC8 * 512), F32, waits=ev_aoe_all_prev)
        dump("attnT", attnT2[:], (128, 16 * 512), BF16, waits=(ev_at_b[1][15],))
        k.flush(f"h1b_{seq}")
        if upto == "h1b" and seq == 0:
            return nc, dbg_outs

        ev_wstr = k.dma("gpsimd", wstr[:].rearrange("p (c o) -> p c o", c=NC8),
                        w_inT_d.ap()[:, 0:D].rearrange("(c p) o -> p c o", p=128), GD)
        ev_rlog = k.dma("sync", rlog[:], rl_d.ap(), Y, waits=tuple(ev_rlw))
        ev_xl = [None, None]
        prev_xn_used = [None, None]
        prev_qf_evac = [None, None]
        for tc in range(NTC):
            par = tc % 2
            ev_xl[par] = k.dma(
                "sync", xn4[:, par * NC8 * 512:(par + 1) * NC8 * 512]
                .rearrange("p (c t) -> p c t", c=NC8),
                xn_d.ap().rearrange("p (c t) -> p c t", c=NC8)[:, :, tc * 512:(tc + 1) * 512],
                Y, waits=(ev_xnw[seq][tc], prev_xn_used[par]))
            last_mm = None
            for ofc in range(NC8):
                psq = PS[4 + ofc % 2]
                evq = None
                for c in range(NC8):
                    evq = k.op("tensor", lambda e, psq=psq, c=c, ofc=ofc, par=par: e.matmul(
                        psq[:], wstr[:, c * D + ofc * 128: c * D + ofc * 128 + 128],
                        xn4[:, par * NC8 * 512 + c * 512: par * NC8 * 512 + c * 512 + 512],
                        start=(c == 0), stop=(c == NC8 - 1)),
                        waits=(ev_xl[par], ev_wstr, prev_qf_evac[ofc % 2]), inc=T)
                last_mm = evq
                prev_qf_evac[ofc % 2] = k.op(
                    "vector", lambda e, psq=psq, ofc=ofc, tc=tc: e.tensor_copy(
                        qf[:, ofc * S + tc * 512: ofc * S + tc * 512 + 512], psq[:]),
                    waits=(evq,), inc=V)
            prev_xn_used[par] = last_mm
        ev_gtm = [None] * NTILE
        prev_gtm = None
        for j in range(NTILE):
            e1 = k.op("vector", lambda e, j=j: e.reduce_max(
                r4[:, 0:1], rlog[:, j * 16:(j + 1) * 16], axis=mybir.AxisListType.X),
                waits=(ev_rlog,), inc=V)
            e2 = k.op("vector", lambda e, j=j: e.tensor_scalar(
                ttmp[:], rlog[:, j * 16:(j + 1) * 16], r4[:, 0:1], None, ALU.is_equal),
                waits=(e1,), inc=V)
            e3 = k.op("vector", lambda e, j=j: e.scalar_tensor_tensor(
                ttmp[:], ttmp[:], -1.0e30, rlog[:, j * 16:(j + 1) * 16], ALU.mult, ALU.add),
                waits=(e2,), inc=V)
            e4 = k.op("vector", lambda e: e.reduce_max(
                r4[:, 1:2], ttmp[:], axis=mybir.AxisListType.X), waits=(e3,), inc=V)
            e5 = k.op("vector", lambda e, j=j: e.tensor_scalar(
                ttmp[:], rlog[:, j * 16:(j + 1) * 16], r4[:, 1:2], None, ALU.is_ge),
                waits=(e4,), inc=V)
            e6 = k.op("scalar", lambda e, j=j: e.activation(
                stmp2[:], rlog[:, j * 16:(j + 1) * 16], AF.Sigmoid),
                waits=(ev_rlog, prev_gtm), inc=SC)
            ev_gtm[j] = k.op("vector", lambda e, j=j: e.tensor_tensor(
                gtm[:, j * 16:(j + 1) * 16], ttmp[:], stmp2[:], ALU.mult),
                waits=(e5, e6), inc=V)
            prev_gtm = ev_gtm[j]
        ev_gfm = [None] * NTILE
        prev_tr = None
        for j in range(NTILE):
            etr = k.op("tensor", lambda e, j=j: e.transpose(
                ps_tr[0:16, 0:128], gtm[:, j * 16:(j + 1) * 16], idn[:]),
                waits=(ev_gtm[j], prev_tr), inc=T)
            ev_gfm[j] = k.op("vector", lambda e, j=j: e.tensor_copy(
                gfm[:, j * 128:(j + 1) * 128], ps_tr[0:16, 0:128]), waits=(etr,), inc=V)
            prev_tr = ev_gfm[j]
        ev_gw = k.dma("sync", g_d.ap(), gfm[:], SD, waits=tuple(ev_gfm))
        dump("g_d", gfm[:], (16, S), BF16, waits=(ev_gw,))

        ev_khl = [None, None]
        ev_vhl = [None, None]
        ev_gbl2 = [None, None]
        ev_odone_h = [None, None]
        ev_gelu_done = [prev_qf_evac[0], prev_qf_evac[1], None]
        ev_ffe_prev = ()
        ev_omm_done_e = [None, None]
        pend_o = None

        def emit_o_block(h, ei, pe, par, ev_ap):
            nonlocal ev_ffe_prev, ev_omm_done_e, ev_odone_h
            ev_omm = None
            for sc in range(2):
                for qc4 in range(NTC):
                    ev_omm = k.op("tensor", lambda e, sc=sc, qc4=qc4, ei=ei, pe=pe, par=par: e.matmul(
                        PS[qc4][:], vh[:, par * NE * 256 + ei * 256 + sc * 128:
                                       par * NE * 256 + ei * 256 + sc * 128 + 128],
                        a_sb[:, pe * 2 * S + sc * S + qc4 * 512:
                             pe * 2 * S + sc * S + qc4 * 512 + 512],
                        start=(ei == 0 and sc == 0), stop=(ei == NE - 1 and sc == 1)),
                        waits=(ev_ap[(sc, qc4)], ev_vhl[par], ev_ffe_prev), inc=T)
            ev_omm_done_e[pe] = ev_omm
            if ei == NE - 1:
                ev_ffe = []
                for qc4 in range(NTC):
                    ev_ffe.append(k.op("vector", lambda e, qc4=qc4, h=h: e.tensor_copy(
                        ffwd[:, h * S + qc4 * 512: h * S + qc4 * 512 + 512], PS[qc4][:]),
                        waits=(ev_omm,), inc=V))
                ev_ffe_prev = tuple(ev_ffe)
                ev_odone_h[par] = ev_ffe[-1]

        for h in range(NH):
            par = h % 2
            ev_khl[par] = k.dma(
                "gpsimd", kh[:, par * NE * SE:(par + 1) * NE * SE]
                .rearrange("p (e s) -> p e s", e=NE),
                kT_d.ap()[h].rearrange("e p s -> p e s"), GD,
                waits=(ev_odone_h[par],))
            ev_vhl[par] = k.dma(
                "gpsimd", vh[:, par * NE * 256:(par + 1) * NE * 256]
                .rearrange("p (e sc d) -> p e sc d", e=NE, sc=2),
                vT_d.ap()[h].rearrange("e (sc p) d -> p e sc d", p=128), GD,
                waits=(ev_odone_h[par],))
            sct = 0
            for ei in range(NE):
                pe = ei % 2
                ev_gbl2[pe] = k.dma(
                    "sync", g_b[:, pe * S:(pe + 1) * S],
                    g_d.ap()[ei:ei + 1, :].to_broadcast((128, S)), Y,
                    waits=(ev_gw, ev_omm_done_e[pe]))
                ev_ap = {}
                for sc in range(2):
                    for qc4 in range(NTC):
                        pss = PS[4 + sct % 3]
                        ev_s = k.op("tensor", lambda e, pss=pss, sc=sc, qc4=qc4, ei=ei, h=h, par=par: e.matmul(
                            pss[:], kh[:, par * NE * SE + ei * SE + sc * 128:
                                       par * NE * SE + ei * SE + sc * 128 + 128],
                            qf[:, h * S + qc4 * 512: h * S + qc4 * 512 + 512],
                            start=True, stop=True),
                            waits=(ev_khl[par], ev_gelu_done[sct % 3], ev_ffe_prev), inc=T)
                        ev_gl = k.op("scalar", lambda e, pss=pss, sc=sc, qc4=qc4, pe=pe: e.activation(
                            a_sb[:, pe * 2 * S + sc * S + qc4 * 512:
                                 pe * 2 * S + sc * S + qc4 * 512 + 512], pss[:], AF.Gelu),
                            waits=(ev_s, ev_omm_done_e[pe]), inc=SC)
                        ev_gelu_done[sct % 3] = ev_gl
                        ev_ap[(sc, qc4)] = k.op("vector", lambda e, sc=sc, qc4=qc4, pe=pe: e.tensor_tensor(
                            a_sb[:, pe * 2 * S + sc * S + qc4 * 512:
                                 pe * 2 * S + sc * S + qc4 * 512 + 512],
                            a_sb[:, pe * 2 * S + sc * S + qc4 * 512:
                                 pe * 2 * S + sc * S + qc4 * 512 + 512],
                            g_b[:, pe * S + qc4 * 512: pe * S + qc4 * 512 + 512],
                            ALU.mult), waits=(ev_gl, ev_gbl2[pe]), inc=V)
                        sct += 1
                if pend_o is not None:
                    emit_o_block(*pend_o)
                pend_o = (h, ei, pe, par, ev_ap)
        emit_o_block(*pend_o)
        dump("qf", qf[:], (128, NH * S), BF16, waits=ev_ffe_prev[:1])
        dump("ffwd", ffwd[:], (128, NH * S), BF16, waits=ev_ffe_prev)
        k.flush(f"p4_{seq}")
        if upto == "p4" and seq == 0:
            return nc, dbg_outs

        def proj_rope(dst, col0, ev_prev_done):
            ev_wls = []
            for hh in range(NH):
                ev_wls.append(k.dma(
                    "gpsimd", wka[:, hh * D:(hh + 1) * D].rearrange("p (c o) -> p c o", c=NC8),
                    w_inT_d.ap()[:, col0 + hh * 128: col0 + hh * 128 + 128]
                    .rearrange("(c p) o -> p c o", p=128), GD,
                    waits=tuple(ev_prev_done)))
            ev_xl5 = [None, None]
            gate0 = ev_prev_done[0] if ev_prev_done else None
            prev_xn5_used = [gate0, gate0]
            prev_psum = [None] * 4
            prev_chain = [None, None]
            ci = 0
            ev_done = []
            for tc in range(NTC):
                par = tc % 2
                ev_xl5[par] = k.dma(
                    "sync", xn5[:, par * NC8 * 512:(par + 1) * NC8 * 512]
                    .rearrange("p (c t) -> p c t", c=NC8),
                    xn_d.ap().rearrange("p (c t) -> p c t", c=NC8)[:, :, tc * 512:(tc + 1) * 512],
                    Y, waits=(ev_xnw[seq][tc], prev_xn5_used[par]))
                last_mm = None
                for h in range(NH):
                    psp = PS[h % 4]
                    sl = slice((ci % 2) * 512, (ci % 2) * 512 + 512)
                    evp = None
                    for c in range(NC8):
                        evp = k.op("tensor", lambda e, psp=psp, c=c, h=h, par=par: e.matmul(
                            psp[:], wka[:, h * D + c * 128: h * D + c * 128 + 128],
                            xn5[:, par * NC8 * 512 + c * 512: par * NC8 * 512 + c * 512 + 512],
                            start=(c == 0), stop=(c == NC8 - 1)),
                            waits=(ev_xl5[par], ev_wls[h], prev_psum[h % 4]), inc=T)
                    last_mm = evp
                    ev_kr = k.op("scalar", lambda e, psp=psp, sl=sl: e.activation(
                        kraw[:, sl], psp[:], AF.Copy),
                        waits=(evp, prev_chain[ci % 2]), inc=SC)
                    es1 = k.op("scalar", lambda e, psp=psp, sl=sl: e.activation(
                        ksw[0:64, sl], psp[64:128, :], AF.Copy),
                        waits=(evp, prev_chain[ci % 2]), inc=SC)
                    es2 = k.op("scalar", lambda e, psp=psp, sl=sl: e.activation(
                        ksw[64:128, sl], psp[0:64, :], AF.Copy),
                        waits=(evp, prev_chain[ci % 2]), inc=SC)
                    prev_psum[h % 4] = es2
                    ev_X = k.op("vector", lambda e, tc=tc, sl=sl: e.tensor_tensor(
                        xs_r[:, sl], kraw[:, sl], tbl_c[:, tc * 512:(tc + 1) * 512],
                        ALU.mult), waits=(ev_kr,), inc=V)
                    ev_Y = k.op("vector", lambda e, tc=tc, sl=sl: e.tensor_tensor(
                        kraw[:, sl], ksw[:, sl], tbl_s[:, tc * 512:(tc + 1) * 512],
                        ALU.mult), waits=(es1, es2, ev_X), inc=V)
                    ev_o = k.op("vector", lambda e, h=h, tc=tc, sl=sl: e.tensor_tensor(
                        dst[:, h * S + tc * 512: h * S + tc * 512 + 512],
                        xs_r[:, sl], kraw[:, sl], ALU.add), waits=(ev_Y,), inc=V)
                    prev_chain[ci % 2] = ev_o
                    ev_done.append(ev_o)
                    ci += 1
                prev_xn5_used[par] = last_mm
            return ev_done

        ev_t1 = k.dma("gpsimd", tbl_c[:], c_cosk.ap(), GD)
        ev_t2 = k.dma("gpsimd", tbl_s[:], c_sink.ap(), GD)
        kv = k.op("vector", lambda e: e.nop(), waits=(ev_t1, ev_t2), inc=V)
        ev_ka_done = proj_rope(ka, 2048, (kv,))
        kv = k.op("vector", lambda e: e.nop(), waits=tuple(ev_ka_done[-1:]), inc=V)
        ev_qa_done = proj_rope(qa, 1024, (kv,))

        ev_wva = k.dma("gpsimd", wva[:].rearrange("p (c o) -> p c o", c=NC8),
                       w_inT_d.ap()[:, 3072:4096].rearrange("(c p) o -> p c o", p=128), GD,
                       waits=tuple(ev_qa_done[-1:]))
        ev_va = [[None, None] for _ in range(NTILE)]
        ev_xl5 = [None, None]
        prev_xn5v = [None, None]
        prev_vae = [None, None]
        for tc in range(NTC):
            par = tc % 2
            ev_xl5[par] = k.dma(
                "sync", xn5[:, par * NC8 * 512:(par + 1) * NC8 * 512]
                .rearrange("p (c t) -> p c t", c=NC8),
                xn_d.ap().rearrange("p (c t) -> p c t", c=NC8)[:, :, tc * 512:(tc + 1) * 512],
                Y, waits=(ev_xnw[seq][tc], prev_xn5v[par], tuple(ev_qa_done[-1:])))
            last_mm = None
            for jji in range(4):
                j = tc * 4 + jji
                for nh in range(2):
                    psv = PS[2 + nh]
                    evv = None
                    for c in range(NC8):
                        evv = k.op("tensor", lambda e, psv=psv, c=c, jji=jji, nh=nh, par=par: e.matmul(
                            psv[:], xn5[:, par * NC8 * 512 + c * 512 + jji * 128:
                                        par * NC8 * 512 + c * 512 + jji * 128 + 128],
                            wva[:, c * D + nh * 512: c * D + nh * 512 + 512],
                            start=(c == 0), stop=(c == NC8 - 1)),
                            waits=(ev_xl5[par], ev_wva, prev_vae[nh]), inc=T)
                    last_mm = evv
                    pv = k.op("scalar", lambda e, psv=psv, j=j, nh=nh: e.activation(
                        va[:, j * D + nh * 512: j * D + nh * 512 + 512], psv[:], AF.Copy),
                        waits=(evv,), inc=SC)
                    prev_vae[nh] = pv
                    ev_va[j][nh] = pv
            prev_xn5v[par] = last_mm
        dump("ka", ka[:], (128, NH * S), BF16, waits=tuple(ev_ka_done[-1:]))
        dump("qa", qa[:], (128, NH * S), BF16, waits=tuple(ev_qa_done[-1:]))
        dump("va", va[:], (128, NTILE * D), BF16, waits=(prev_vae[1],))
        k.flush(f"p5a_{seq}")
        if upto == "p5a" and seq == 0:
            return nc, dbg_outs

        ev_tri = k.dma("sync", tri_s[:], c_tri.ap(), Y)
        slot_free = [None] * 4
        prev_exp = [None, None]
        PU = [2, 3, 6]
        ev_uev = {2: None, 3: None, 6: None}
        den_guard = [None] * 6
        ev_rcb_used = [None] * 4
        prev_rec = [None] * 4
        last_ue = [None, None]
        pend_evac = None

        def emit_evac(pu, h, qc, hq_, ev_u_, ev_rb2_):
            ev_ue = k.op("vector", lambda e, pu=pu, h=h, qc=qc, hq=hq_: e.tensor_tensor(
                ao2[:, h * S + qc * 512: h * S + qc * 512 + 512], PS[pu][:],
                rcb[:, (hq % 4) * 512:(hq % 4) * 512 + 512], ALU.mult),
                waits=(ev_u_, ev_rb2_), inc=V)
            ev_uev[pu] = ev_ue
            ev_rcb_used[hq_ % 4] = ev_ue
            last_ue[hq_ % 2] = ev_ue

        hq = 0
        for qc in range(NTC):
            nb = 4 * qc + 4
            for h in range(NH):
                pu = PU[hq % 3]
                pd = 4 + (hq % 2)
                drow = 32 * ((hq // 2) % 3)
                ev_dn = None
                ev_u = None
                pend = None
                for b in range(nb + 1):
                    if b < nb:
                        pssc = PS[b % 2]
                        sl4 = b % 4
                        ev_sc2 = k.op("tensor", lambda e, pssc=pssc, b=b, h=h, qc=qc: e.matmul(
                            pssc[:], ka[:, h * S + b * 128: h * S + b * 128 + 128],
                            qa[:, h * S + qc * 512: h * S + qc * 512 + 512],
                            start=True, stop=True),
                            waits=(prev_exp[b % 2], ev_uev[pu]), inc=T)
                        ev_e = k.op("scalar", lambda e, pssc=pssc, sl4=sl4: e.activation(
                            expT[:, sl4 * 512:sl4 * 512 + 512], pssc[:], AF.Exp,
                            scale=float(1.0 / math.sqrt(DH))),
                            waits=(ev_sc2, slot_free[sl4]), inc=SC)
                        prev_exp[b % 2] = ev_e
                        if b >= nb - 4:
                            r = b - (nb - 4)
                            ev_e = k.op("vector", lambda e, sl4=sl4, r=r: e.tensor_tensor(
                                expT[:, sl4 * 512:sl4 * 512 + 512],
                                expT[:, sl4 * 512:sl4 * 512 + 512],
                                tri_s[:, r * 512:(r + 1) * 512], ALU.mult),
                                waits=(ev_e, ev_tri), inc=V)
                        cur = (b, sl4, ev_e)
                    else:
                        cur = None
                    if pend is not None:
                        pb, psl4, pev_e = pend
                        ev_dn = k.op("tensor", lambda e, pd=pd, drow=drow, sl4=psl4, b=pb, nb=nb: e.matmul(
                            PS[pd][drow:drow + 1, :], ones[:, 0:1],
                            expT[:, sl4 * 512:sl4 * 512 + 512],
                            start=(b == 0), stop=(b == nb - 1)),
                            waits=(pev_e, den_guard[hq % 6] if pb == 0 else None), inc=T)
                        ev_u = k.op("tensor", lambda e, pu=pu, sl4=psl4, b=pb, nb=nb, h=h: e.matmul(
                            PS[pu][:], va[:, b * D + h * 128: b * D + h * 128 + 128],
                            expT[:, sl4 * 512:sl4 * 512 + 512],
                            start=(b == 0), stop=(b == nb - 1)),
                            waits=(pev_e, ev_va[pb][h // 4]), inc=T)
                        slot_free[psl4] = ev_u
                    pend = cur
                if pend_evac is not None:
                    emit_evac(*pend_evac)
                rrow = 32 * (hq % 4)
                ev_rc = k.op("vector", lambda e, pd=pd, drow=drow, rrow=rrow: e.reciprocal(
                    recs[rrow:rrow + 1, :], PS[pd][drow:drow + 1, :]),
                    waits=(ev_dn, prev_rec[hq % 4]), inc=V)
                den_guard[hq % 6] = ev_rc
                ev_rw = k.dma("sync", rc_d.ap()[hq % 4:hq % 4 + 1, :],
                              recs[rrow:rrow + 1, :], SD, waits=(ev_rc,))
                ev_rb2 = k.dma("sync", rcb[:, (hq % 4) * 512:(hq % 4) * 512 + 512],
                               rc_d.ap()[hq % 4:hq % 4 + 1, :].to_broadcast((128, 512)), Y,
                               waits=(ev_rw, ev_rcb_used[hq % 4]))
                prev_rec[hq % 4] = ev_rb2
                pend_evac = (pu, h, qc, hq, ev_u, ev_rb2)
                hq += 1
        emit_evac(*pend_evac)
        dump("ao2", ao2[:], (128, NH * S), BF16, waits=(last_ue[0], last_ue[1]))
        k.flush(f"p5b_{seq}")
        if upto == "p5b" and seq == 0:
            return nc, dbg_outs

        ev_wol = [None] * NC8
        for ec in range(NC8):
            ev_wol[ec] = k.dma(
                "gpsimd", wo_s[:, ec * 16 * 128:(ec + 1) * 16 * 128]
                .rearrange("p (cc o) -> p cc o", cc=16),
                w_outT_d.ap()[:, ec * 128:(ec + 1) * 128]
                .rearrange("(cc p) o -> p cc o", p=128), GD)
        prev_oq = [None, None]
        prev_fin_p6 = ()
        prev_app6 = ()
        ev_out_final = None
        for qc in range(NTC):
            ev_xq = k.dma(
                "sync", xq[:].rearrange("p (c t) -> p c t", c=NC8),
                xsp_d.ap().rearrange("p (c t) -> p c t", c=NC8)[:, :, qc * 512:(qc + 1) * 512],
                Y, waits=(ev_xsw[seq][qc], ev_out_final))
            ev_oqe = [None] * NC8
            for ec in range(NC8):
                pso = PS[ec % 2]
                evo = None
                for cc in range(16):
                    src = ao2 if cc < 8 else ffwd
                    evo = k.op("tensor", lambda e, pso=pso, cc=cc, src=src, qc=qc, ec=ec: e.matmul(
                        pso[:], wo_s[:, ec * 16 * 128 + cc * 128: ec * 16 * 128 + cc * 128 + 128],
                        src[:, (cc % 8) * S + qc * 512: (cc % 8) * S + qc * 512 + 512],
                        start=(cc == 0), stop=(cc == 15)),
                        waits=(ev_wol[ec], prev_oq[ec % 2]), inc=T)
                ev_oqe[ec] = k.op("vector", lambda e, pso=pso, ec=ec: e.tensor_copy(
                    outq[:, ec * 512:(ec + 1) * 512], pso[:]), waits=(evo,), inc=V)
                prev_oq[ec % 2] = ev_oqe[ec]
            ev_p6 = None
            for ec in range(NC8):
                ev_sq = k.op("vector", lambda e, ec=ec: e.tensor_tensor(
                    sq6[:], outq[:, ec * 512:(ec + 1) * 512],
                    outq[:, ec * 512:(ec + 1) * 512], ALU.mult),
                    waits=(ev_oqe[ec], ev_p6), inc=V)
                k.op("tensor", lambda e, ec=ec: e.matmul(
                    PS[6][0:1, :], ones[:, 0:1], outq[:, ec * 512:(ec + 1) * 512],
                    start=(ec == 0), stop=(ec == NC8 - 1)),
                    waits=(ev_oqe[ec],) + prev_fin_p6, inc=T)
                ev_p6 = k.op("tensor", lambda e, ec=ec: e.matmul(
                    PS[6][32:33, :], ones[:, 0:1], sq6[:],
                    start=(ec == 0), stop=(ec == NC8 - 1)),
                    waits=(ev_sq,) + prev_fin_p6, inc=T)
            ev_m, ev_r, ev_pf = stat_finalize(PS[6], 0, 32, waits=(ev_p6,))
            ev_mb, ev_rb = bcast_psum(PS[6], PS[5], ev_m, ev_r,
                                      free_m=(ev_pf,) + prev_app6, free_r=prev_app6)
            ev_sts = [None] * NC8
            ev_t26l = None
            for c in range(NC8):
                k.op("vector", lambda e, c=c: e.tensor_tensor(
                    t16[:], outq[:, c * 512:(c + 1) * 512], PS[6][:], ALU.subtract),
                    waits=(ev_mb,), inc=V)
                ev_t26l = k.op("vector", lambda e: e.tensor_tensor(
                    t26[:], t16[:], PS[5][:], ALU.mult), waits=(ev_rb,), inc=V)
                ev_sts[c] = k.op("vector", lambda e, c=c: e.tensor_tensor(
                    xq[:, c * 512:(c + 1) * 512], xq[:, c * 512:(c + 1) * 512], t26[:],
                    ALU.add), waits=(ev_xq,), inc=V)
            prev_fin_p6 = (ev_m, ev_pf, ev_t26l)
            prev_app6 = (ev_t26l,)
            ev_out_final = k.dma(
                "sync",
                out_t.ap()[seq].rearrange("(c p) t -> p c t", p=128)[:, :, qc * 512:(qc + 1) * 512],
                xq[:].rearrange("p (c t) -> p c t", c=NC8), SD, waits=tuple(ev_sts))
        if seq + 1 < NSEQ:
            ev_x_pend[seq + 1] = [
                k.dma("sync", xres[:, c * S:(c + 1) * S],
                      x_t.ap()[seq + 1, c * 128:(c + 1) * 128, :], Y)
                for c in range(NC8)]
        dump("outq", outq[:], (128, NC8 * 512), BF16, waits=(prev_oq[0], prev_oq[1]))
        dump("st6", stat[:], (1, 1536), F32, waits=(prev_fin_p6[0], prev_fin_p6[1]))
        dump("xq6", xq[:], (128, NC8 * 512), F32, waits=(ev_out_final,))
        k.op("sync", lambda e: e.nop(), waits=(ev_out_final,))
        k.flush(f"p6_{seq}")

    allsems = [sm for pl in k.pools.values() for sm in pl["sems"]]
    k.op("scalar", lambda e: e.nop(),
         waits=tuple((sm, k.cnt[sm]) for sm in allsems))
    k.flush("fin")
    return nc, dbg_outs


def host_prep(inputs):
    g = lambda n: np.asarray(inputs[n], dtype=np.float32)
    x = g("L0_attn_x")
    common = {
        "wqkT": np.ascontiguousarray(g("L0_attn_Wqk")),
        "bqk": np.ascontiguousarray(g("L0_attn_bqk")),
        "gam": np.ascontiguousarray(g("L0_attn_gamma")),
        "bet": np.ascontiguousarray(g("L0_attn_beta")),
        "wvT": np.ascontiguousarray(g("L0_attn_Wv")),
        "bv": np.ascontiguousarray(g("L0_attn_bv")),
        "rel": np.ascontiguousarray(g("L0_attn_rel_emb")),
        "w_inT": np.ascontiguousarray(g("L0_ffn_w_in").T),
        "w_outT": np.ascontiguousarray(g("L0_ffn_w_out").T),
        "kT": np.ascontiguousarray(g("L0_ffn_k_ffwd").transpose(1, 0, 3, 2)),
        "vT": np.ascontiguousarray(g("L0_ffn_v_ffwd").transpose(1, 0, 2, 3)),
    }
    common.update(host_constants())
    in_maps = []
    for c in range(NCORES):
        m = dict(common)
        m["x_t"] = np.ascontiguousarray(x[2 * c:2 * c + 2].transpose(0, 2, 1))
        in_maps.append(m)
    return in_maps


def kernel(**inputs):
    nc, _ = build_program()
    in_maps = host_prep(inputs)
    res = run_bass_kernel_spmd(nc, in_maps, core_ids=list(range(NCORES)))
    out = np.empty((16, S, D), dtype=np.float32)
    for c in range(NCORES):
        out[2 * c:2 * c + 2] = res.results[c]["out_t"].transpose(0, 2, 1)
    return out
